# version 17
# baseline (speedup 1.0000x reference)
import contextlib
import os
import numpy as np
import ml_dtypes
import concourse.bass as bass
import concourse.mybir as mybir
from concourse.bass_utils import run_bass_kernel_spmd

F32 = mybir.dt.float32
BF16 = mybir.dt.bfloat16
U32 = mybir.dt.uint32
AF = mybir.ActivationFunctionType
ALU = mybir.AluOpType
AX = mybir.AxisListType

H = 4096
DA = 1024
S = 4096
TOK = 1024
INTER = 11008
NGU = 172
NDT = 32
NIC = 86
NKC = 32


def build_nc(nphases=8, debug=False):
    nc = bass.Bass("TRN2", target_bir_lowering=False)

    x_perm = nc.dram_tensor("x_perm", [S, H], F32, kind="ExternalInput").ap()
    wqkv = nc.dram_tensor("wqkv", [H, 3 * DA], F32, kind="ExternalInput").ap()
    bqkv_t = nc.dram_tensor("bqkv_t", [128, 24], F32, kind="ExternalInput").ap()
    bqv = nc.dram_tensor("bqv", [1, DA], F32, kind="ExternalInput").ap()
    wo = nc.dram_tensor("wo", [DA, H], F32, kind="ExternalInput").ap()
    bo_in = nc.dram_tensor("bo_in", [1, H], F32, kind="ExternalInput").ap()
    qwt_gu = nc.dram_tensor("qwt_gu", [2 * INTER, 512], U32, kind="ExternalInput").ap()
    sct_gu = nc.dram_tensor("sct_gu", [128, NGU * 32], F32, kind="ExternalInput").ap()
    qz_gu = nc.dram_tensor("qz_gu", [32, 2752], U32, kind="ExternalInput").ap()
    sc_gu = nc.dram_tensor("sc_gu", [32, 2 * INTER], F32, kind="ExternalInput").ap()
    qwt_d = nc.dram_tensor("qwt_d", [H, 1376], U32, kind="ExternalInput").ap()
    sct_d = nc.dram_tensor("sct_d", [128, NDT * 86], F32, kind="ExternalInput").ap()
    qz_d = nc.dram_tensor("qz_d", [86, 512], U32, kind="ExternalInput").ap()
    sc_d = nc.dram_tensor("sc_d", [86, H], F32, kind="ExternalInput").ap()
    esel_in = nc.dram_tensor("esel_in", [128, 32 * 32], BF16, kind="ExternalInput").ap()
    eseld_in = nc.dram_tensor("eseld_in", [128, 86 * 86], BF16, kind="ExternalInput").ap()
    ident_in = nc.dram_tensor("ident_in", [128, 128], BF16, kind="ExternalInput").ap()
    ones_in = nc.dram_tensor("ones_in", [1, 128], BF16, kind="ExternalInput").ap()

    y = nc.dram_tensor("y", [TOK, H], F32, kind="ExternalOutput").ap()

    dbg = "ExternalOutput" if debug else "Internal"
    xnT_dram = nc.dram_tensor("xnT_dram", [H, S], BF16, kind=dbg).ap()
    kT_dram = nc.dram_tensor("kT_dram", [DA, S], BF16, kind=dbg).ap()
    v_dram = nc.dram_tensor("v_dram", [S, DA], BF16, kind=dbg).ap()
    s_dram = nc.dram_tensor("s_dram", [TOK, H], BF16, kind=dbg).ap()
    wtgu_dram = nc.dram_tensor("wtgu_dram", [2 * INTER, H], BF16, kind=dbg).ap()
    wtd_dram = nc.dram_tensor("wtd_dram", [H, INTER], BF16, kind=dbg).ap()
    hT_dram = nc.dram_tensor("hT_dram", [INTER, TOK], BF16, kind=dbg).ap()
    offgu_dram = nc.dram_tensor("offgu_dram", [32, 2 * INTER], BF16, kind=dbg).ap()
    offd_dram = nc.dram_tensor("offd_dram", [86, H], BF16, kind="Internal").ap()
    bobc_dram = nc.dram_tensor("bobc_dram", [128, H], BF16, kind="Internal").ap()
    if debug:
        rsx_dbg = nc.dram_tensor("rsx_dbg", [32, TOK], BF16, kind="ExternalOutput").ap()
        qt_dbg = nc.dram_tensor("qt_dbg", [128, 8 * TOK], BF16, kind="ExternalOutput").ap()
        pt_dbg = nc.dram_tensor("pt_dbg", [128, 8 * S], BF16, kind="ExternalOutput").ap()
        ot_dbg = nc.dram_tensor("ot_dbg", [128, 8 * TOK], BF16, kind="ExternalOutput").ap()

    with contextlib.ExitStack() as top:
        e = top.enter_context
        ident = e(nc.sbuf_tensor("identS", [128, 128], BF16))
        ones1 = e(nc.sbuf_tensor("onesS", [1, 128], BF16))
        esel_sb = e(nc.sbuf_tensor("eselS", [128, 32 * 32], BF16))
        bqkv_sb = e(nc.sbuf_tensor("bqkvS", [128, 24], F32))
        vb_bc = e(nc.sbuf_tensor("vb_bc", [128, DA], BF16))
        rsx = e(nc.sbuf_tensor("rsx", [32, TOK], BF16))

        sem_names = ["vch", "ld0", "lb0", "lsc0", "lsc0b", "lsct4", "lsct4d", "pe0", "ev0", "vz0", "gz0",
                     "ld1b", "gst1b", "lx2b", "l4b", "l4db", "l5wb",
                     "loffA", "loffB", "l6sb",
                     "gz0b", "l4c", "l4d2", "per6i", "l6r0", "l6r1",
                     "l6w0", "l6w1", "l6w2", "l6w3",
                     "ld1", "ac1", "vvar1", "vxn1", "toct1", "voct1", "trs1", "ars1", "gst1",
                     "lw2", "lx2", "pe2", "a2", "g2",
                     "lw2b", "pe2b", "a2b", "gq2",
                     "lw2c", "pe2c", "v2c", "g2c",
                     "lk3", "pe3", "v3", "vm3", "a3", "vn3", "t3", "vo3",
                     "lv3", "pe3b", "v3b",
                     "lo3", "pe3c", "v3c", "g3",
                     "l4", "v4", "g4", "l4d", "v4d", "g4d",
                     "l5x", "l5w", "pe5", "a5", "v5", "g5",
                     "l6h", "l6e", "pe6r", "a6r", "pe6", "peic6", "a6", "l6s", "v6", "g6"]
        sems = {n: e(nc.semaphore(n)) for n in sem_names}
        g = lambda n: sems[n]
        vch_n = [0]

        with contextlib.ExitStack() as ph:
            pe_ = ph.enter_context
            bo_bf = pe_(nc.sbuf_tensor("bo_bf", [1, H], BF16))
            bqv_bf = pe_(nc.sbuf_tensor("bqv_bf", [1, DA], BF16))
            qzg_sb = pe_(nc.sbuf_tensor("qzg", [32, 2752], U32))
            qzd_sb = pe_(nc.sbuf_tensor("qzd", [86, 512], U32))
            zq_ch = pe_(nc.sbuf_tensor("zq_ch", [32, 5504], U32))
            scg_ch = [pe_(nc.sbuf_tensor(f"scg{i}", [32, 5504], F32)) for i in range(2)]
            offst = [pe_(nc.sbuf_tensor(f"offst{i}", [32, 5504], BF16)) for i in range(2)]
            zq_d = pe_(nc.sbuf_tensor("zq_d", [86, H], U32))
            scd_sb = pe_(nc.sbuf_tensor("scd", [86, H], F32))
            off_d = pe_(nc.sbuf_tensor("off_d0", [86, H], BF16))
            bo_bc = pe_(nc.sbuf_tensor("bo_bc0", [128, H], BF16))
            psb = pe_(nc.psum_tensor("psb", [128, 512], F32))
            ld0, lb0, pe0, ev0, vz0, gz0 = (g(k) for k in ["ld0", "lb0", "pe0", "ev0", "vz0", "gz0"])

            with nc.Block() as blk:
                @blk.sync
                def _(sync):
                    for dst, src in [(ident, ident_in), (ones1, ones_in),
                                     (esel_sb, esel_in), (bqkv_sb, bqkv_t),
                                     (qzg_sb, qz_gu), (qzd_sb, qz_d),
                                     (scd_sb, sc_d)]:
                        sync.dma_start(out=dst[:, :], in_=src).then_inc(ld0, 16)
                    for ch in range(4):
                        if ch >= 2:
                            sync.wait_ge(vz0, ch - 1)
                        sync.dma_start(out=scg_ch[ch % 2][:, :],
                                       in_=sc_gu[:, ch * 5504:(ch + 1) * 5504]).then_inc(
                                           g("lsc0") if ch % 2 == 0 else g("lsc0b"), 16)

                @blk.gpsimd
                def _(gp):
                    gp.dma_start(out=bo_bf[:, :], in_=bo_in).then_inc(lb0, 16)
                    gp.dma_start(out=bqv_bf[:, :], in_=bqv).then_inc(lb0, 16)
                    for ch in range(4):
                        gp.wait_ge(vz0, ch + 1)
                        gp.dma_start(out=offgu_dram[:, ch * 5504:(ch + 1) * 5504],
                                     in_=offst[ch % 2][:, :]).then_inc(
                                         gz0 if ch % 2 == 0 else g("gz0b"), 16)
                    gp.wait_ge(vz0, 5)
                    gp.dma_start(out=offd_dram, in_=off_d[:, :]).then_inc(gz0, 16)
                    gp.wait_ge(ev0, 8)
                    gp.dma_start(out=bobc_dram, in_=bo_bc[:, :]).then_inc(gz0, 16)
                    gp.wait_ge(gz0, 16 * 4)
                    gp.wait_ge(g("gz0b"), 16 * 2)

                @blk.tensor
                def _(t):
                    t.wait_ge(ld0, 16 * 7)
                    t.wait_ge(lb0, 16 * 2)
                    for i in range(8):
                        if i >= 1:
                            t.wait_ge(ev0, i)
                        t.matmul(psb[:, :], ones1[:, :], bo_bf[:, i * 512:(i + 1) * 512],
                                 start=True, stop=True).then_inc(pe0, 1)
                    for i in range(2):
                        t.wait_ge(ev0, 8 + i)
                        t.matmul(psb[:, :], ones1[:, :], bqv_bf[:, i * 512:(i + 1) * 512],
                                 start=True, stop=True).then_inc(pe0, 1)

                @blk.vector
                def _(v):
                    for i in range(8):
                        v.wait_ge(pe0, i + 1)
                        v.tensor_copy(bo_bc[:, i * 512:(i + 1) * 512], psb[:, :]).then_inc(ev0, 1)
                    for i in range(2):
                        v.wait_ge(pe0, 9 + i)
                        v.tensor_copy(vb_bc[:, i * 512:(i + 1) * 512], psb[:, :]).then_inc(ev0, 1)
                    v.wait_ge(ld0, 16 * 7)
                    for ch in range(4):
                        v.wait_ge(g("lsc0") if ch % 2 == 0 else g("lsc0b"), 16 * (ch // 2 + 1))
                        if ch >= 1:
                            v.wait_ge(vz0, ch)
                        for s_ in range(8):
                            v.tensor_scalar(
                                bass.AP(zq_ch, s_, [[5504, 32], [8, 688]]),
                                qzg_sb[:, ch * 688:(ch + 1) * 688], 4 * s_, 15,
                                op0=ALU.logical_shift_right, op1=ALU.bitwise_and)
                        if ch >= 2:
                            v.wait_ge(gz0 if ch % 2 == 0 else g("gz0b"), 16 * (ch // 2))
                        v.scalar_tensor_tensor(offst[ch % 2][:, :], zq_ch[:, :], 1.0,
                                               scg_ch[ch % 2][:, :],
                                               op0=ALU.add, op1=ALU.mult).then_inc(vz0, 1)
                    for s_ in range(8):
                        v.tensor_scalar(
                            bass.AP(zq_d, s_, [[H, 86], [8, 512]]),
                            qzd_sb[:, :], 4 * s_, 15,
                            op0=ALU.logical_shift_right, op1=ALU.bitwise_and)
                    v.scalar_tensor_tensor(off_d[:, :], zq_d[:, :], 1.0, scd_sb[:, :],
                                           op0=ALU.add, op1=ALU.mult).then_inc(vz0, 1)

        if nphases < 1:
            return nc

        p2deq = contextlib.ExitStack()
        de = p2deq.enter_context
        sct_gu_sb = de(nc.sbuf_tensor("sct_guS", [128, NGU * 32], F32, side="right"))
        qbuf_g = [de(nc.sbuf_tensor(f"qbg{i}", [128, 256], U32, side="right")) for i in range(4)]
        wqu_g = de(nc.sbuf_tensor("wqu_g", [128, 2048], U32, side="right"))
        wbf_g = [de(nc.sbuf_tensor(f"wbg{i}", [128, 2048], BF16, side="right")) for i in range(3)]
        NHT = 2 * NGU
        DEQA = os.environ.get("KDEQ", "abc")

        def gu_row(jt):
            if jt < 86:
                return (jt // 2) * 512 + (jt % 2) * 128
            ju = jt - 86
            return (ju // 2) * 512 + 256 + (ju % 2) * 128

        L4S = ["l4", "l4b", "l4c", "l4d2"]

        def deq_gu_v(v, ht):
            jt, hf = divmod(ht, 2)
            v.wait_ge(g("lsct4"), 16)
            v.wait_ge(g(L4S[ht % 4]), 16 * (ht // 4 + 1))
            if ht >= 1:
                v.wait_ge(g("v4"), ht)
            for s_ in range(8):
                v.tensor_scalar(bass.AP(wqu_g, s_, [[2048, 128], [8, 256]]),
                                qbuf_g[ht % 4][:, :], 4 * s_, 15,
                                op0=ALU.logical_shift_right, op1=ALU.bitwise_and)
            if ht >= 3:
                v.wait_ge(g("g4"), 16 * (ht - 2))
            sc_b = bass.AP(sct_gu_sb, jt * 32 + hf * 16,
                           [[NGU * 32, 128], [1, 16], [0, 128]])
            v.tensor_tensor(wbf_g[ht % 3][:, :], wqu_g[:, :], sc_b,
                            op=ALU.mult).then_inc(g("v4"), 1)

        gu_cur = [0]
        def deq_gu_gp_pairs(gp, n, cap=None):
            for _ in range(n):
                h = gu_cur[0]
                if h >= NHT or (cap is not None and h >= cap):
                    return
                for hl in (list(range(4)) if h == 0 else [h + 3]):
                    if hl < NHT:
                        jt, hf = divmod(hl, 2)
                        if hl >= 4:
                            gp.wait_ge(g("v4"), hl - 3)
                        gp.dma_start(
                            out=qbuf_g[hl % 4][:, :],
                            in_=qwt_gu[jt * 128:(jt + 1) * 128,
                                       hf * 256:(hf + 1) * 256]).then_inc(g(L4S[hl % 4]), 16)
                jt, hf = divmod(h, 2)
                gp.wait_ge(g("v4"), h + 1)
                row = gu_row(jt)
                gp.dma_start(out=wtgu_dram[row:row + 128, hf * 2048:(hf + 1) * 2048],
                             in_=wbf_g[h % 3][:, :]).then_inc(g("g4"), 16)
                gu_cur[0] += 1


        with contextlib.ExitStack() as ph:
            pe_ = ph.enter_context
            xbuf = [pe_(nc.sbuf_tensor(f"xb{i}", [128, H], F32)) for i in range(2)]
            xnbf = [pe_(nc.sbuf_tensor(f"xn{i}", [128, H], BF16)) for i in range(2)]
            sq_scr = pe_(nc.sbuf_tensor("sq_scr", [128, H], BF16))
            stat = pe_(nc.sbuf_tensor("stat", [128, 8], F32))
            xstage = [pe_(nc.sbuf_tensor(f"xst{i}", [128, 1024], BF16)) for i in range(2)]
            ps_tr = pe_(nc.psum_tensor("ps_tr1", [128, 1024], BF16))
            ps_rs = pe_(nc.psum_tensor("ps_rs1", [32, TOK], F32))
            ld1, ac1 = g("ld1"), g("ac1")
            vvar1, vxn1, toct1 = g("vvar1"), g("vxn1"), g("toct1")
            voct1, trs1, ars1, gst1 = g("voct1"), g("trs1"), g("ars1"), g("gst1")
            NT = 32

            with nc.Block() as blk:
                @blk.sync
                def _(sync):
                    for tt in range(NT):
                        if tt >= 2:
                            sync.wait_ge(vxn1, tt - 1)
                        sync.dma_start(out=xbuf[tt % 2][:, :],
                                       in_=x_perm[tt * 128:(tt + 1) * 128, :]).then_inc(
                                           ld1 if tt % 2 == 0 else g("ld1b"), 16)

                @blk.scalar
                def _(sca):
                    for tt in range(NT):
                        p = tt % 2
                        sca.wait_ge(ld1 if p == 0 else g("ld1b"), 16 * (tt // 2 + 1))
                        sca.activation(sq_scr[:, :], xbuf[p][:, :], AF.Square,
                                       accum_out=stat[:, 1 + p:2 + p]).then_inc(ac1, 1)
                        sca.wait_ge(vvar1, tt + 1)
                        sca.activation(stat[:, 6:7], stat[:, 4 + p:5 + p],
                                       AF.Sqrt).then_inc(ac1, 1)
                        if tt < 8:
                            sca.wait_ge(trs1, 4 * (tt + 1))
                            sca.mul(rsx[:, tt * 128:(tt + 1) * 128],
                                    ps_rs[:, tt * 128:(tt + 1) * 128], -1.0).then_inc(ars1, 1)

                @blk.vector
                def _(v):
                    dq1_cur = [0]
                    def ch_(ins):
                        vch_n[0] += 1
                        ins.then_inc(g("vch"), 1)
                        v.wait_ge(g("vch"), vch_n[0])
                    for tt in range(NT):
                        p = tt % 2
                        v.wait_ge(ld1 if p == 0 else g("ld1b"), 16 * (tt // 2 + 1))
                        if tt >= 1:
                            v.wait_ge(vxn1, tt)
                        ch_(v.reduce_sum(stat[:, 0:1], xbuf[p][:, :], axis=AX.X))
                        ch_(v.tensor_scalar(stat[:, 3:4], stat[:, 0:1], 1.0 / H, None, op0=ALU.mult))
                        v.wait_ge(ac1, 2 * tt + 1)
                        ch_(v.tensor_scalar(stat[:, 1 + p:2 + p], stat[:, 1 + p:2 + p], 1.0 / H, None,
                                            op0=ALU.mult))
                        ch_(v.tensor_tensor(stat[:, 4 + p:5 + p], stat[:, 3:4], stat[:, 3:4], op=ALU.mult))
                        ch_(v.tensor_tensor(stat[:, 4 + p:5 + p], stat[:, 1 + p:2 + p],
                                            stat[:, 4 + p:5 + p], op=ALU.subtract))
                        v.tensor_scalar(stat[:, 4 + p:5 + p], stat[:, 4 + p:5 + p], 1e-5, None,
                                        op0=ALU.add).then_inc(vvar1, 1)
                        v.wait_ge(ac1, 2 * tt + 2)
                        ch_(v.reciprocal(stat[:, 7:8], stat[:, 6:7]))
                        if tt >= 2:
                            v.wait_ge(toct1, 4 * (tt - 2) + 4)
                        v.tensor_scalar(xnbf[p][:, :], xbuf[p][:, :], stat[:, 3:4], stat[:, 7:8],
                                        op0=ALU.subtract, op1=ALU.mult).then_inc(vxn1, 1)
                        for o in range(4):
                            q = tt * 4 + o
                            v.wait_ge(toct1, q + 1)
                            if q >= 2:
                                v.wait_ge(gst1 if q % 2 == 0 else g("gst1b"), 16 * (q // 2))
                                need = min(q - 1, 32)
                                if need > 0:
                                    v.wait_ge(trs1, need)
                            v.tensor_copy(xstage[q % 2][:, :], ps_tr[:, :]).then_inc(voct1, 1)
                        while dq1_cur[0] < 3 * (tt + 1):
                            deq_gu_v(v, dq1_cur[0])
                            dq1_cur[0] += 1

                @blk.tensor
                def _(t):
                    for tt in range(NT):
                        t.wait_ge(vxn1, tt + 1)
                        for o in range(4):
                            q = tt * 4 + o
                            if q >= 1:
                                t.wait_ge(voct1, q)
                            for j in range(8):
                                kc = o * 8 + j
                                mm = t.transpose(ps_tr[:, j * 128:(j + 1) * 128],
                                                 xnbf[tt % 2][:, kc * 128:(kc + 1) * 128],
                                                 ident[:, :])
                            mm.then_inc(toct1, 1)
                            if tt < 8:
                                t.wait_ge(voct1, q + 1)
                                if o == 0 and tt >= 1:
                                    t.wait_ge(ars1, tt)
                                for j in range(8):
                                    kc = o * 8 + j
                                    mm = t.matmul(ps_rs[:, tt * 128:(tt + 1) * 128],
                                                  esel_sb[:, kc * 32:(kc + 1) * 32],
                                                  xstage[q % 2][:, j * 128:(j + 1) * 128],
                                                  start=(kc == 0), stop=(kc == 31))
                                mm.then_inc(trs1, 1)

                @blk.gpsimd
                def _(gp):
                    gp.dma_start(out=sct_gu_sb[:, :], in_=sct_gu).then_inc(g("lsct4"), 16)
                    for tt in range(NT):
                        for o in range(4):
                            q = tt * 4 + o
                            gp.wait_ge(voct1, q + 1)
                            dram_ap = bass.AP(xnT_dram.tensor,
                                              (o * 8) * 128 * S + tt * 128,
                                              [[S, 128], [128 * S, 8], [1, 128]])
                            gp.dma_start(out=dram_ap, in_=xstage[q % 2][:, :]).then_inc(
                                gst1 if q % 2 == 0 else g("gst1b"), 16)
                        deq_gu_gp_pairs(gp, 3, cap=96)
                    deq_gu_gp_pairs(gp, NHT, cap=96)
                    gp.wait_ge(gst1, 16 * NT * 2)
                    gp.wait_ge(g("gst1b"), 16 * NT * 2)
                    if debug:
                        gp.wait_ge(ars1, 8)
                        gp.dma_start(out=rsx_dbg, in_=rsx[:, :]).then_inc(gst1, 16)
                        gp.wait_ge(gst1, 16 * (NT * 2 + 1))

        if nphases < 2:
            return nc

        with contextlib.ExitStack() as attn:
            ae = attn.enter_context
            qt_all = ae(nc.sbuf_tensor("qt_all", [128, 8 * TOK], BF16))

            with contextlib.ExitStack() as ph:
                pe_ = ph.enter_context
                wk_sb = pe_(nc.sbuf_tensor("wk_sb", [128, NKC * 1024], BF16))
                xsl = [pe_(nc.sbuf_tensor(f"xsl{i}", [128, NKC * 512], BF16)) for i in range(2)]
                kst = [pe_(nc.sbuf_tensor(f"kst{i}", [128, 512], BF16)) for i in range(2)]
                ps_k = [pe_(nc.psum_tensor(f"ps_k{i}", [128, 512], F32)) for i in range(2)]
                lw2, lx2, pe2, a2, g2 = g("lw2"), g("lx2"), g("pe2"), g("a2"), g("g2")

                with nc.Block() as blk:
                    @blk.gpsimd
                    def _(gp):
                        for kc in range(NKC):
                            src = bass.AP(wqkv.tensor, kc * 128 * 3072 + 1024,
                                          [[3072, 128], [1, 1024]])
                            gp.dma_start(out=wk_sb[:, kc * 1024:(kc + 1) * 1024],
                                         in_=src).then_inc(lw2, 16)
                        if "a" in DEQA:
                            deq_gu_gp_pairs(gp, 8, cap=224)
                        for idx in range(64):
                            tb, cb = divmod(idx, 8)
                            gp.wait_ge(a2, idx + 1)
                            gp.dma_start(
                                out=bass.AP(kT_dram.tensor, cb * 128 * S + tb * 512,
                                            [[S, 128], [1, 512]]),
                                in_=kst[idx % 2][:, :]).then_inc(g2, 16)
                            if "a" in DEQA:
                                deq_gu_gp_pairs(gp, 3, cap=224)
                        gp.wait_ge(g2, 16 * 64)
                        deq_gu_gp_pairs(gp, NHT, cap=224)

                    @blk.sync
                    def _(sync):
                        for tb in range(8):
                            if tb >= 2:
                                sync.wait_ge(pe2, (tb - 1) * 8)
                            for kc in range(NKC):
                                src = bass.AP(xnT_dram.tensor, kc * 128 * S + tb * 512,
                                              [[S, 128], [1, 512]])
                                sync.dma_start(out=xsl[tb % 2][:, kc * 512:(kc + 1) * 512],
                                               in_=src).then_inc(lx2 if tb % 2 == 0 else g("lx2b"), 16)

                    @blk.tensor
                    def _(t):
                        t.wait_ge(lw2, 16 * NKC)
                        for tb in range(8):
                            t.wait_ge(lx2 if tb % 2 == 0 else g("lx2b"), 16 * NKC * (tb // 2 + 1))
                            for cb in range(8):
                                idx = tb * 8 + cb
                                if idx >= 2:
                                    t.wait_ge(a2, idx - 1)
                                for kc in range(NKC):
                                    mm = t.matmul(
                                        ps_k[idx % 2][:, :],
                                        wk_sb[:, kc * 1024 + cb * 128:kc * 1024 + (cb + 1) * 128],
                                        xsl[tb % 2][:, kc * 512:(kc + 1) * 512],
                                        start=(kc == 0), stop=(kc == NKC - 1))
                                mm.then_inc(pe2, 1)

                    @blk.scalar
                    def _(sca):
                        for idx in range(64):
                            tb, cb = divmod(idx, 8)
                            sca.wait_ge(pe2, idx + 1)
                            if idx >= 2:
                                sca.wait_ge(g2, 16 * (idx - 1))
                            sca.activation(kst[idx % 2][:, :], ps_k[idx % 2][:, :], AF.Identity,
                                           bias=bqkv_sb[:, 8 + cb:9 + cb],
                                           scale=1.0).then_inc(a2, 1)

                    @blk.vector
                    def _(v):
                        if "a" in DEQA:
                            for ht in range(96, 224):
                                deq_gu_v(v, ht)

            with contextlib.ExitStack() as ph:
                pe_ = ph.enter_context
                wq_sb = pe_(nc.sbuf_tensor("wq_sb", [128, NKC * 1024], BF16))
                xsl = [pe_(nc.sbuf_tensor(f"xsq{i}", [128, NKC * 512], BF16)) for i in range(2)]
                ps_q = [pe_(nc.psum_tensor(f"ps_q{i}", [128, 512], F32)) for i in range(2)]
                lw2b, pe2b, a2b, gq2 = g("lw2b"), g("pe2b"), g("a2b"), g("gq2")

                with nc.Block() as blk:
                    @blk.gpsimd
                    def _(gp):
                        for kc in range(NKC):
                            src = bass.AP(wqkv.tensor, kc * 128 * 3072,
                                          [[3072, 128], [1, 1024]])
                            gp.dma_start(out=wq_sb[:, kc * 1024:(kc + 1) * 1024],
                                         in_=src).then_inc(lw2b, 16)
                        if "b" in DEQA:
                            deq_gu_gp_pairs(gp, NHT, cap=272)
                        if debug:
                            gp.wait_ge(a2b, 16)
                            gp.dma_start(out=qt_dbg, in_=qt_all[:, :]).then_inc(gq2, 16)
                            gp.wait_ge(gq2, 16)

                    @blk.sync
                    def _(sync):
                        for tb in range(2):
                            for kc in range(NKC):
                                src = bass.AP(xnT_dram.tensor, kc * 128 * S + tb * 512,
                                              [[S, 128], [1, 512]])
                                sync.dma_start(out=xsl[tb][:, kc * 512:(kc + 1) * 512],
                                               in_=src).then_inc(lw2b, 16)

                    @blk.tensor
                    def _(t):
                        t.wait_ge(lw2b, 16 * NKC * 3)
                        for tb in range(2):
                            for cb in range(8):
                                idx = tb * 8 + cb
                                if idx >= 2:
                                    t.wait_ge(a2b, idx - 1)
                                for kc in range(NKC):
                                    mm = t.matmul(
                                        ps_q[idx % 2][:, :],
                                        wq_sb[:, kc * 1024 + cb * 128:kc * 1024 + (cb + 1) * 128],
                                        xsl[tb][:, kc * 512:(kc + 1) * 512],
                                        start=(kc == 0), stop=(kc == NKC - 1))
                                mm.then_inc(pe2b, 1)

                    @blk.scalar
                    def _(sca):
                        for idx in range(16):
                            tb, cb = divmod(idx, 8)
                            sca.wait_ge(pe2b, idx + 1)
                            sca.activation(
                                qt_all[:, cb * 1024 + tb * 512:cb * 1024 + (tb + 1) * 512],
                                ps_q[idx % 2][:, :], AF.Identity,
                                bias=bqkv_sb[:, cb:cb + 1], scale=1.0).then_inc(a2b, 1)

                    @blk.vector
                    def _(v):
                        if "b" in DEQA:
                            for ht in range(224, 272):
                                deq_gu_v(v, ht)

            with contextlib.ExitStack() as ph:
                pe_ = ph.enter_context
                wv_sb = pe_(nc.sbuf_tensor("wv_sb", [128, NKC * 1024], BF16))
                xsl = [pe_(nc.sbuf_tensor(f"xsv{i}", [128, NKC * 512], BF16)) for i in range(2)]
                vst = [pe_(nc.sbuf_tensor(f"vst{i}", [128, 1024], BF16)) for i in range(2)]
                ps_v = [pe_(nc.psum_tensor(f"ps_v{i}", [128, 1024], F32)) for i in range(2)]
                lw2c, pe2c, v2c, g2c = g("lw2c"), g("pe2c"), g("v2c"), g("g2c")
                LXB = 16 * NKC * 4

                with nc.Block() as blk:
                    @blk.gpsimd
                    def _(gp):
                        for kc in range(NKC):
                            src = bass.AP(wqkv.tensor, kc * 128 * 3072 + 2048,
                                          [[3072, 128], [1, 1024]])
                            gp.dma_start(out=wv_sb[:, kc * 1024:(kc + 1) * 1024],
                                         in_=src).then_inc(lw2c, 16)
                        for idx in range(32):
                            gp.wait_ge(v2c, idx + 1)
                            gp.dma_start(out=v_dram[idx * 128:(idx + 1) * 128, :],
                                         in_=vst[idx % 2][:, :]).then_inc(g2c, 16)
                            if "c" in DEQA:
                                hi = 272 + ((idx + 1) * 72) // 32
                                deq_gu_gp_pairs(gp, hi - gu_cur[0])
                        if "c" in DEQA:
                            gp.wait_ge(g("g4"), 16 * gu_cur[0])
                        gp.wait_ge(g2c, 16 * 32)

                    @blk.sync
                    def _(sync):
                        for tb in range(8):
                            if tb >= 2:
                                sync.wait_ge(pe2c, (tb - 1) * 4)
                            for kc in range(NKC):
                                src = bass.AP(xnT_dram.tensor, kc * 128 * S + tb * 512,
                                              [[S, 128], [1, 512]])
                                sync.dma_start(out=xsl[tb % 2][:, kc * 512:(kc + 1) * 512],
                                               in_=src).then_inc(g("lx2") if tb % 2 == 0 else g("lx2b"), 16)

                    @blk.tensor
                    def _(t):
                        t.wait_ge(lw2c, 16 * NKC)
                        for tb in range(8):
                            t.wait_ge(g("lx2") if tb % 2 == 0 else g("lx2b"),
                                      LXB + 16 * NKC * (tb // 2 + 1))
                            for tv in range(4):
                                idx = tb * 4 + tv
                                if idx >= 2:
                                    t.wait_ge(v2c, idx - 1)
                                for kc in range(NKC):
                                    for dh in range(2):
                                        mm = t.matmul(
                                            ps_v[idx % 2][:, dh * 512:(dh + 1) * 512],
                                            xsl[tb % 2][:, kc * 512 + tv * 128:kc * 512 + (tv + 1) * 128],
                                            wv_sb[:, kc * 1024 + dh * 512:kc * 1024 + (dh + 1) * 512],
                                            start=(kc == 0), stop=(kc == NKC - 1))
                                mm.then_inc(pe2c, 1)

                    @blk.vector
                    def _(v):
                        ht_next = [272]
                        for idx in range(32):
                            v.wait_ge(pe2c, idx + 1)
                            if idx >= 2:
                                v.wait_ge(g2c, 16 * (idx - 1))
                            v.tensor_tensor(vst[idx % 2][:, :], ps_v[idx % 2][:, :],
                                            vb_bc[:, :], op=ALU.add).then_inc(v2c, 1)
                            if "c" in DEQA:
                                hi = 272 + ((idx + 1) * 72) // 32
                                while ht_next[0] < hi:
                                    deq_gu_v(v, ht_next[0])
                                    ht_next[0] += 1

            p2deq.close()

            if nphases < 3:
                return nc

            pt_all = ae(nc.sbuf_tensor("pt_all", [128, 8 * S], BF16))
            with contextlib.ExitStack() as ph:
                pe_ = ph.enter_context
                kt_sb = pe_(nc.sbuf_tensor("kt_sb", [128, 8 * S], BF16))
                s_f = pe_(nc.sbuf_tensor("s_f", [128, S], F32))
                p_bf = pe_(nc.sbuf_tensor("p_bf", [128, S], BF16))
                p2_bf = pe_(nc.sbuf_tensor("p2bf", [128, S], BF16))
                sm = pe_(nc.sbuf_tensor("sm3", [128, 4], F32))
                ps_s = [pe_(nc.psum_tensor(f"ps_s{i}", [128, 512], F32)) for i in range(2)]
                ps_t3 = pe_(nc.psum_tensor("ps_t3", [128, 1024], BF16))
                lk3, pe3, v3, vm3 = g("lk3"), g("pe3"), g("v3"), g("vm3")
                a3, vn3, t3, vo3 = g("a3"), g("vn3"), g("t3"), g("vo3")

                with nc.Block() as blk:
                    @blk.sync
                    def _(sync):
                        for dc in range(8):
                            sync.dma_start(out=kt_sb[:, dc * S:(dc + 1) * S],
                                           in_=kT_dram[dc * 128:(dc + 1) * 128, :]).then_inc(lk3, 16)

                    @blk.tensor
                    def _(t):
                        t.wait_ge(lk3, 16 * 8)
                        def _scores(t, qt, kbs):
                            for kb in kbs:
                                idx = qt * 8 + kb
                                if idx >= 2:
                                    t.wait_ge(v3, idx - 1)
                                for dc in range(8):
                                    mm = t.matmul(
                                        ps_s[idx % 2][:, :],
                                        qt_all[:, dc * 1024 + qt * 128:dc * 1024 + (qt + 1) * 128],
                                        kt_sb[:, dc * S + kb * 512:dc * S + (kb + 1) * 512],
                                        start=(dc == 0), stop=(dc == 7))
                                mm.then_inc(pe3, 1)
                        def _pt(t, qp):
                            t.wait_ge(vn3, qp + 1)
                            for o in range(4):
                                qq = qp * 4 + o
                                if qq >= 1:
                                    t.wait_ge(vo3, qq)
                                for j in range(8):
                                    kc = o * 8 + j
                                    mm = t.transpose(ps_t3[:, j * 128:(j + 1) * 128],
                                                     p2_bf[:, kc * 128:(kc + 1) * 128],
                                                     ident[:, :])
                                mm.then_inc(t3, 1)
                        for qt in range(9):
                            if qt < 8:
                                _scores(t, qt, range(0, 2))
                            if qt >= 1:
                                _pt(t, qt - 1)
                            if qt < 8:
                                _scores(t, qt, range(2, 8))

                    @blk.vector
                    def _(v):
                        def ch3(ins):
                            vch_n[0] += 1
                            ins.then_inc(g("vch"), 1)
                            v.wait_ge(g("vch"), vch_n[0])
                        for qt in range(8):
                            for kb in range(8):
                                idx = qt * 8 + kb
                                v.wait_ge(pe3, idx + 1)
                                if kb == 0 and qt >= 1:
                                    v.wait_ge(a3, qt)
                                v.tensor_copy(s_f[:, kb * 512:(kb + 1) * 512],
                                              ps_s[idx % 2][:, :]).then_inc(v3, 1)
                            ch3(v.reduce_max(sm[:, 0:1], s_f[:, :], axis=AX.X))
                            v.tensor_scalar(sm[:, 1:2], sm[:, 0:1], -1.0 / 32.0, None,
                                            op0=ALU.mult).then_inc(vm3, 1)
                            v.wait_ge(a3, qt + 1)
                            ch3(v.reciprocal(sm[:, 3:4], sm[:, 2:3]))
                            if qt >= 1:
                                v.wait_ge(t3, 4 * qt)
                            v.tensor_scalar(p2_bf[:, :], p_bf[:, :],
                                            sm[:, 3:4], None, op0=ALU.mult).then_inc(vn3, 1)
                            for o in range(4):
                                qq = qt * 4 + o
                                v.wait_ge(t3, qq + 1)
                                out_ap = bass.AP(pt_all, (o * 8) * 1024 + qt * 128,
                                                 [[8 * S, 128], [1024, 8], [1, 128]])
                                v.tensor_copy(out_ap, ps_t3[:, :]).then_inc(vo3, 1)

                    @blk.scalar
                    def _(sca):
                        for qt in range(8):
                            sca.wait_ge(vm3, qt + 1)
                            if qt >= 1:
                                sca.wait_ge(vn3, qt)
                            sca.activation(p_bf[:, :], s_f[:, :], AF.Exp,
                                           bias=sm[:, 1:2], scale=1.0 / 32.0,
                                           accum_out=sm[:, 2:3]).then_inc(a3, 1)

            if debug:
                with nc.Block() as blk:
                    @blk.gpsimd
                    def _(gp):
                        gp.dma_start(out=pt_dbg, in_=pt_all[:, :]).then_inc(g("gq2"), 16)
                        gp.wait_ge(g("gq2"), 32 if debug else 16)

            if nphases < 4:
                return nc

            ot_all = ae(nc.sbuf_tensor("ot_all", [128, 8 * TOK], BF16))

            with contextlib.ExitStack() as ph:
                pe_ = ph.enter_context
                v_sb = pe_(nc.sbuf_tensor("v_sb", [128, NKC * 1024], BF16))
                ps_av = [pe_(nc.psum_tensor(f"ps_av{i}", [128, 512], F32)) for i in range(2)]
                lv3, pe3b, v3b = g("lv3"), g("pe3b"), g("v3b")

                with nc.Block() as blk:
                    @blk.sync
                    def _(sync):
                        for kc in range(NKC):
                            sync.dma_start(out=v_sb[:, kc * 1024:(kc + 1) * 1024],
                                           in_=v_dram[kc * 128:(kc + 1) * 128, :]).then_inc(lv3, 16)

                    @blk.tensor
                    def _(t):
                        t.wait_ge(lv3, 16 * NKC)
                        for db in range(8):
                            for qh in range(2):
                                idx = db * 2 + qh
                                if idx >= 2:
                                    t.wait_ge(v3b, idx - 1)
                                for kc in range(NKC):
                                    mm = t.matmul(
                                        ps_av[idx % 2][:, :],
                                        v_sb[:, kc * 1024 + db * 128:kc * 1024 + (db + 1) * 128],
                                        pt_all[:, kc * 1024 + qh * 512:kc * 1024 + (qh + 1) * 512],
                                        start=(kc == 0), stop=(kc == NKC - 1))
                                mm.then_inc(pe3b, 1)

                    @blk.vector
                    def _(v):
                        for idx in range(16):
                            db, qh = divmod(idx, 2)
                            v.wait_ge(pe3b, idx + 1)
                            v.tensor_copy(ot_all[:, db * 1024 + qh * 512:db * 1024 + (qh + 1) * 512],
                                          ps_av[idx % 2][:, :]).then_inc(v3b, 1)

            if debug:
                with nc.Block() as blk:
                    @blk.gpsimd
                    def _(gp):
                        gp.dma_start(out=ot_dbg, in_=ot_all[:, :]).then_inc(g("gq2"), 16)
                        gp.wait_ge(g("gq2"), 48)

            if nphases < 5:
                return nc

            with contextlib.ExitStack() as ph:
                pe_ = ph.enter_context
                wo_sb = pe_(nc.sbuf_tensor("wo_sb", [128, 8 * H], BF16))
                bo_bc = pe_(nc.sbuf_tensor("bo_bc3", [128, H], BF16))
                sst = [pe_(nc.sbuf_tensor(f"sst{i}", [128, 512], BF16)) for i in range(2)]
                ps_wo = [pe_(nc.psum_tensor(f"ps_wo{i}", [128, 512], F32)) for i in range(2)]
                lo3, pe3c, v3c, g3 = g("lo3"), g("pe3c"), g("v3c"), g("g3")

                with nc.Block() as blk:
                    @blk.gpsimd
                    def _(gp):
                        gp.dma_start(out=bo_bc[:, :], in_=bobc_dram).then_inc(lo3, 16)
                        for dc in range(8):
                            src = bass.AP(wo.tensor, dc * 128 * H, [[H, 128], [1, H]])
                            gp.dma_start(out=wo_sb[:, dc * H:(dc + 1) * H],
                                         in_=src).then_inc(lo3, 16)
                        for idx in range(64):
                            qt, hb = divmod(idx, 8)
                            gp.wait_ge(v3c, idx + 1)
                            gp.dma_start(
                                out=bass.AP(s_dram.tensor, qt * 128 * H + hb * 512,
                                            [[H, 128], [1, 512]]),
                                in_=sst[idx % 2][:, :]).then_inc(g3, 16)
                        gp.wait_ge(g3, 16 * 64)

                    @blk.tensor
                    def _(t):
                        t.wait_ge(lo3, 16 * 9)
                        for qt in range(8):
                            for hb in range(8):
                                idx = qt * 8 + hb
                                if idx >= 2:
                                    t.wait_ge(v3c, idx - 1)
                                for dc in range(8):
                                    mm = t.matmul(
                                        ps_wo[idx % 2][:, :],
                                        ot_all[:, dc * TOK + qt * 128:dc * TOK + (qt + 1) * 128],
                                        wo_sb[:, dc * H + hb * 512:dc * H + (hb + 1) * 512],
                                        start=(dc == 0), stop=(dc == 7))
                                mm.then_inc(pe3c, 1)

                    @blk.vector
                    def _(v):
                        for idx in range(64):
                            qt, hb = divmod(idx, 8)
                            v.wait_ge(pe3c, idx + 1)
                            if idx >= 2:
                                v.wait_ge(g3, 16 * (idx - 1))
                            v.scalar_tensor_tensor(sst[idx % 2][:, :], ps_wo[idx % 2][:, :],
                                                   1.0, bo_bc[:, hb * 512:(hb + 1) * 512],
                                                   op0=ALU.mult, op1=ALU.add).then_inc(v3c, 1)

        if nphases < 7:
            return nc

        with contextlib.ExitStack() as ph:
            pe_ = ph.enter_context
            xn5 = pe_(nc.sbuf_tensor("xn5", [128, NKC * 1024], BF16))
            wb5 = [pe_(nc.sbuf_tensor(f"wb5{i}", [128, NKC * 512], BF16)) for i in range(2)]
            offs = [pe_(nc.sbuf_tensor(f"offs{i}", [32, 512], BF16)) for i in range(2)]
            sg5 = pe_(nc.sbuf_tensor("sg5", [128, 1024], BF16))
            hst = [pe_(nc.sbuf_tensor(f"hst{i}", [128, 1024], BF16)) for i in range(2)]
            sct_d_sb = pe_(nc.sbuf_tensor("sct_dS", [128, NDT * 86], F32))
            qbuf_d = [pe_(nc.sbuf_tensor(f"qbd{i}", [128, 352], U32)) for i in range(2)]
            wqu_d = pe_(nc.sbuf_tensor("wqu_d", [128, 2816], U32))
            wbf_d = [pe_(nc.sbuf_tensor(f"wbd{i}", [128, 2816], BF16)) for i in range(2)]
            ps5 = [pe_(nc.psum_tensor(f"ps5_{i}", [128, 512], F32)) for i in range(8)]
            l5x, l5w, pe5 = g("l5x"), g("l5w"), g("pe5")
            a5, v5, g5 = g("a5"), g("v5"), g("g5")
            NI = 43
            NQD = 4 * NDT
            QW = [0, 352, 704, 1056, 1376]
            QG = [0, 22, 44, 66, 86]

            def deq_d_v(v, q):
                dt, qq = divmod(q, 4)
                par = q % 2
                nw = QW[qq + 1] - QW[qq]
                ng = QG[qq + 1] - QG[qq]
                v.wait_ge(g("lsct4d"), 16)
                v.wait_ge(g("l4d") if par == 0 else g("l4db"), 16 * (q // 2 + 1))
                if q >= 1:
                    v.wait_ge(g("v4d"), q)
                for s_ in range(8):
                    v.tensor_scalar(bass.AP(wqu_d, s_, [[2816, 128], [8, nw]]),
                                    qbuf_d[par][:, 0:nw], 4 * s_, 15,
                                    op0=ALU.logical_shift_right, op1=ALU.bitwise_and)
                if q >= 2:
                    v.wait_ge(g("g4d"), 16 * (q - 1))
                sc_b = bass.AP(sct_d_sb, dt * 86 + QG[qq],
                               [[NDT * 86, 128], [1, ng], [0, 128]])
                v.tensor_tensor(wbf_d[par][:, 0:nw * 8], wqu_d[:, 0:nw * 8], sc_b,
                                op=ALU.mult).then_inc(g("v4d"), 1)

            d_cur = [0]
            def deq_d_gp_pairs(gp, n):
                for _ in range(n):
                    qd = d_cur[0]
                    if qd >= NQD:
                        return
                    for ql in ([0, 1] if qd == 0 else [qd + 1]):
                        if ql < NQD:
                            dt, qq = divmod(ql, 4)
                            if ql >= 2:
                                gp.wait_ge(g("v4d"), ql - 1)
                            gp.dma_start(
                                out=qbuf_d[ql % 2][:, 0:QW[qq + 1] - QW[qq]],
                                in_=qwt_d[dt * 128:(dt + 1) * 128,
                                          QW[qq]:QW[qq + 1]]).then_inc(
                                    g("l4d") if ql % 2 == 0 else g("l4db"), 16)
                    dt, qq = divmod(qd, 4)
                    gp.wait_ge(g("v4d"), qd + 1)
                    gp.dma_start(out=wtd_dram[dt * 128:(dt + 1) * 128,
                                              QW[qq] * 8:QW[qq + 1] * 8],
                                 in_=wbf_d[qd % 2][:, 0:(QW[qq + 1] - QW[qq]) * 8]).then_inc(
                                     g("g4d"), 16)
                    d_cur[0] += 1

            with nc.Block() as blk:
                @blk.sync
                def _(sync):
                    sync.dma_start(out=sct_d_sb[:, :], in_=sct_d).then_inc(g("lsct4d"), 16)
                    for kc in range(NKC):
                        sync.dma_start(out=xn5[:, kc * 1024:(kc + 1) * 1024],
                                       in_=xnT_dram[kc * 128:(kc + 1) * 128, 0:1024]).then_inc(l5x, 16)
                    for i in range(NI):
                        if i >= 2:
                            sync.wait_ge(pe5, 2 * (i - 1))
                        for kc in range(NKC):
                            gsrc = bass.AP(wtgu_dram.tensor, i * 512 * H + kc * 128,
                                           [[H, 512], [1, 128]])
                            sync.dma_start_transpose(
                                wb5[i % 2][:, kc * 512:(kc + 1) * 512], gsrc).then_inc(
                                    l5w if i % 2 == 0 else g("l5wb"), 16)

                @blk.scalar
                def _(sca):
                    for i in range(min(2, NI)):
                        lsem = g(["loffA", "loffB"][i % 2])
                        sca.dma_start(out=offs[i % 2][:, 0:256],
                                      in_=offgu_dram[:, (2 * i) * 128:(2 * i + 2) * 128]).then_inc(lsem, 16)
                        sca.dma_start(out=offs[i % 2][:, 256:512],
                                      in_=offgu_dram[:, INTER + (2 * i) * 128:INTER + (2 * i + 2) * 128]).then_inc(lsem, 16)
                    for idx in range(2 * NI):
                        i, t2 = divmod(idx, 2)
                        bank = (idx % 2) * 4
                        sca.wait_ge(pe5, idx + 1)
                        if idx >= 1:
                            sca.wait_ge(v5, idx)
                        sca.activation(sg5[:, 0:512], ps5[bank + 0][:, :], AF.Silu)
                        sca.activation(sg5[:, 512:1024], ps5[bank + 1][:, :], AF.Silu).then_inc(a5, 1)
                        if t2 == 1 and i + 2 < NI:
                            j = i + 2
                            lsem = g(["loffA", "loffB"][j % 2])
                            sca.dma_start(out=offs[j % 2][:, 0:256],
                                          in_=offgu_dram[:, (2 * j) * 128:(2 * j + 2) * 128]).then_inc(lsem, 16)
                            sca.dma_start(out=offs[j % 2][:, 256:512],
                                          in_=offgu_dram[:, INTER + (2 * j) * 128:INTER + (2 * j + 2) * 128]).then_inc(lsem, 16)

                @blk.tensor
                def _(t):
                    t.wait_ge(l5x, 16 * NKC)
                    for i in range(NI):
                        t.wait_ge(l5w if i % 2 == 0 else g("l5wb"), 16 * 32 * (i // 2 + 1))
                        t.wait_ge(g(["loffA", "loffB"][i % 2]), 32 * (i // 2 + 1))
                        for t2 in range(2):
                            idx = 2 * i + t2
                            bank = (idx % 2) * 4
                            if idx >= 2:
                                t.wait_ge(a5, idx - 1)
                                t.wait_ge(v5, idx - 1)
                            for kc in range(NKC):
                                for j4 in range(4):
                                    t.matmul(ps5[bank + j4][:, :],
                                             wb5[i % 2][:, kc * 512 + j4 * 128:kc * 512 + (j4 + 1) * 128],
                                             xn5[:, kc * 1024 + t2 * 512:kc * 1024 + (t2 + 1) * 512],
                                             start=(kc == 0), stop=False)
                            for j4 in range(4):
                                mm = t.matmul(ps5[bank + j4][:, :],
                                              offs[i % 2][:, j4 * 128:(j4 + 1) * 128],
                                              rsx[:, t2 * 512:(t2 + 1) * 512],
                                              start=False, stop=True)
                            mm.then_inc(pe5, 1)

                @blk.vector
                def _(v):
                    dv_cur = [0]
                    for idx in range(2 * NI):
                        bank = (idx % 2) * 4
                        v.wait_ge(a5, idx + 1)
                        if idx >= 2:
                            v.wait_ge(g5, 32 * (idx - 1))
                        v.tensor_tensor(hst[idx % 2][:, 0:512], sg5[:, 0:512],
                                        ps5[bank + 2][:, :], op=ALU.mult)
                        v.tensor_tensor(hst[idx % 2][:, 512:1024], sg5[:, 512:1024],
                                        ps5[bank + 3][:, :], op=ALU.mult).then_inc(v5, 1)
                        hi_d = ((idx + 1) * NQD) // (2 * NI)
                        while dv_cur[0] < hi_d:
                            deq_d_v(v, dv_cur[0])
                            dv_cur[0] += 1

                @blk.gpsimd
                def _(gp):
                    for idx in range(2 * NI):
                        i, t2 = divmod(idx, 2)
                        gp.wait_ge(v5, idx + 1)
                        gp.dma_start(
                            out=hT_dram[(2 * i) * 128:(2 * i + 1) * 128, t2 * 512:(t2 + 1) * 512],
                            in_=hst[idx % 2][:, 0:512]).then_inc(g5, 16)
                        gp.dma_start(
                            out=hT_dram[(2 * i + 1) * 128:(2 * i + 2) * 128, t2 * 512:(t2 + 1) * 512],
                            in_=hst[idx % 2][:, 512:1024]).then_inc(g5, 16)
                        hi_d = ((idx + 1) * NQD) // (2 * NI)
                        deq_d_gp_pairs(gp, hi_d - d_cur[0])
                    gp.wait_ge(g5, 32 * 2 * NI)
                    gp.wait_ge(g("g4d"), 16 * NQD)

        if nphases < 8:
            return nc

        with contextlib.ExitStack() as ph:
            pe_ = ph.enter_context
            hT6 = pe_(nc.sbuf_tensor("hT6", [128, NIC * 512], BF16))
            eseld_sb = pe_(nc.sbuf_tensor("eseldS", [128, 86 * 86], BF16))
            off_d = pe_(nc.sbuf_tensor("off_d6", [86, H], BF16))
            rsh_sb = [pe_(nc.sbuf_tensor(f"rsh{i}", [86, 512], BF16)) for i in range(2)]
            wd6 = [pe_(nc.sbuf_tensor(f"wd6{i}", [128, 1024], BF16)) for i in range(6)]
            rsb = [pe_(nc.sbuf_tensor(f"rsb{i}", [128, 512], BF16)) for i in range(2)]
            gb6 = [pe_(nc.sbuf_tensor(f"gb6{i}", [128, 512], BF16)) for i in range(2)]
            xs6 = [pe_(nc.sbuf_tensor(f"xs6{i}", [128, 512], F32)) for i in range(2)]
            ss6 = [pe_(nc.sbuf_tensor(f"ss6{i}", [128, 512], BF16)) for i in range(2)]
            gs6 = pe_(nc.sbuf_tensor("gs6", [128, 512], F32))
            yst = [pe_(nc.sbuf_tensor(f"yst{i}", [128, 512], F32)) for i in range(2)]
            l6h, l6e, pe6r, a6r = g("l6h"), g("l6e"), g("pe6r"), g("a6r")
            pe6, peic6, a6 = g("pe6"), g("peic6"), g("a6")
            l6s, v6, g6 = g("l6s"), g("v6"), g("g6")

            with contextlib.ExitStack() as rscope:
                ps_rsh = [rscope.enter_context(nc.psum_tensor(f"ps_rsh{i}", [86, 512], F32))
                          for i in range(2)]
                with nc.Block() as blk:
                    @blk.sync
                    def _(sync):
                        sync.dma_start(out=eseld_sb[:, :], in_=eseld_in).then_inc(l6e, 16)
                        sync.dma_start(out=off_d[:, :], in_=offd_dram).then_inc(l6e, 16)
                        for ic in range(NIC):
                            sync.dma_start(out=hT6[:, ic * 512:(ic + 1) * 512],
                                           in_=hT_dram[ic * 128:(ic + 1) * 128,
                                                       0:512]).then_inc(l6h, 16)
                        for ic in range(NIC):
                            if ic >= 2:
                                sync.wait_ge(g("per6i"), ic - 1)
                            sync.dma_start(out=rsb[ic % 2][:, :],
                                           in_=hT_dram[ic * 128:(ic + 1) * 128,
                                                       512:1024]).then_inc(g(f"l6r{ic % 2}"), 16)

                    @blk.tensor
                    def _(t):
                        t.wait_ge(l6h, 16 * NIC)
                        t.wait_ge(l6e, 32)
                        for ic in range(NIC):
                            mm = t.matmul(ps_rsh[0][:, :], eseld_sb[:, ic * 86:(ic + 1) * 86],
                                          hT6[:, ic * 512:(ic + 1) * 512],
                                          start=(ic == 0), stop=(ic == NIC - 1))
                        mm.then_inc(pe6r, 1)
                        for ic in range(NIC):
                            t.wait_ge(g(f"l6r{ic % 2}"), 16 * (ic // 2 + 1))
                            mm = t.matmul(ps_rsh[1][:, :], eseld_sb[:, ic * 86:(ic + 1) * 86],
                                          rsb[ic % 2][:, :],
                                          start=(ic == 0), stop=(ic == NIC - 1))
                            mm.then_inc(g("per6i"), 1)

                    @blk.scalar
                    def _(sca):
                        sca.wait_ge(pe6r, 1)
                        sca.mul(rsh_sb[0][:, :], ps_rsh[0][:, :], -1.0).then_inc(a6r, 1)
                        sca.wait_ge(g("per6i"), NIC)
                        sca.mul(rsh_sb[1][:, :], ps_rsh[1][:, :], -1.0).then_inc(a6r, 1)

            ps_d = [pe_(nc.psum_tensor(f"ps_d{i}", [128, 512], F32)) for i in range(8)]

            for t2 in range(2):
                with nc.Block() as blk:
                    @blk.sync
                    def _(sync, t2=t2):
                        if t2 == 1:
                            for ic in range(NIC):
                                sync.dma_start(out=hT6[:, ic * 512:(ic + 1) * 512],
                                               in_=hT_dram[ic * 128:(ic + 1) * 128,
                                                           512:1024]).then_inc(l6h, 16)
                        for hob2 in range(4):
                            for ic in range(NIC):
                                gic = (t2 * 4 + hob2) * NIC + ic
                                if gic >= 6:
                                    sync.wait_ge(peic6, gic - 5)
                                src = bass.AP(wtd_dram.tensor, hob2 * 1024 * INTER + ic * 128,
                                              [[INTER, 1024], [1, 128]])
                                lsem6 = [g("l6w0"), g("l6w1"), g("l6w2"), g("l6w3"),
                                         g("l6r0"), g("l6r1")][gic % 6]
                                sync.dma_start_transpose(wd6[gic % 6][:, :],
                                                         src).then_inc(lsem6, 16)

                    @blk.tensor
                    def _(t, t2=t2):
                        t.wait_ge(l6h, 16 * NIC * (t2 + 1))
                        t.wait_ge(a6r, 2)
                        for hob2 in range(4):
                            for ic in range(NIC):
                                gic = (t2 * 4 + hob2) * NIC + ic
                                rbase = 16 * 43 if gic % 6 >= 4 else 0
                                lsem6 = [g("l6w0"), g("l6w1"), g("l6w2"), g("l6w3"),
                                         g("l6r0"), g("l6r1")][gic % 6]
                                t.wait_ge(lsem6, rbase + 16 * (gic // 6 + 1))
                                for hh in range(2):
                                    ghob = t2 * 8 + hob2 * 2 + hh
                                    for tt in range(4):
                                        if ic == 0 and ghob >= 2:
                                            t.wait_ge(a6, 4 * (ghob - 2) + tt + 1)
                                        mm = t.matmul(
                                            ps_d[hh * 4 + tt][:, :],
                                            hT6[:, ic * 512 + tt * 128:ic * 512 + (tt + 1) * 128],
                                            wd6[gic % 6][:, hh * 512:(hh + 1) * 512],
                                            start=(ic == 0), stop=False)
                                mm.then_inc(peic6, 1)
                            for hh in range(2):
                                hob = hob2 * 2 + hh
                                for tt in range(4):
                                    mm = t.matmul(ps_d[hh * 4 + tt][:, :],
                                                  rsh_sb[t2][:, tt * 128:(tt + 1) * 128],
                                                  off_d[:, hob * 512:(hob + 1) * 512],
                                                  start=False, stop=True)
                                    mm.then_inc(pe6, 1)

                    @blk.scalar
                    def _(sca, t2=t2):
                        for hob2 in range(4):
                            for hh in range(2):
                                hob = hob2 * 2 + hh
                                ghob = t2 * 8 + hob
                                for tt in range(4):
                                    sidx = hob * 4 + tt
                                    gidx = 32 * t2 + sidx
                                    if gidx >= 2:
                                        sca.wait_ge(v6, gidx - 1)
                                    r0 = t2 * 512 + tt * 128
                                    lsem = l6s if gidx % 2 == 0 else g("l6sb")
                                    sca.dma_start(out=xs6[sidx % 2][:, :],
                                                  in_=x_perm[r0:r0 + 128,
                                                             hob * 512:(hob + 1) * 512]).then_inc(lsem, 16)
                                    sca.dma_start(out=ss6[sidx % 2][:, :],
                                                  in_=s_dram[r0:r0 + 128,
                                                             hob * 512:(hob + 1) * 512]).then_inc(lsem, 16)
                                    sca.wait_ge(pe6, 4 * ghob + tt + 1)
                                    sca.activation(gb6[sidx % 2][:, :], ps_d[hh * 4 + tt][:, :],
                                                   AF.Silu).then_inc(a6, 1)

                    @blk.vector
                    def _(v, t2=t2):
                        for hob in range(8):
                            for tt in range(4):
                                sidx = hob * 4 + tt
                                gidx = 32 * t2 + sidx
                                v.wait_ge(a6, gidx + 1)
                                v.wait_ge(l6s if gidx % 2 == 0 else g("l6sb"), 32 * (gidx // 2 + 1))
                                if gidx >= 2:
                                    v.wait_ge(g6, 16 * (gidx - 1))
                                if gidx >= 1:
                                    v.wait_ge(v6, gidx)
                                v.tensor_tensor(gs6[:, :], gb6[sidx % 2][:, :],
                                                ss6[sidx % 2][:, :], op=ALU.mult)
                                v.tensor_tensor(yst[gidx % 2][:, :], gs6[:, :],
                                                xs6[sidx % 2][:, :], op=ALU.add).then_inc(v6, 1)

                    @blk.gpsimd
                    def _(gp, t2=t2):
                        for hob in range(8):
                            for tt in range(4):
                                sidx = hob * 4 + tt
                                gidx = 32 * t2 + sidx
                                gp.wait_ge(v6, gidx + 1)
                                r0 = t2 * 512 + tt * 128
                                gp.dma_start(out=y[r0:r0 + 128, hob * 512:(hob + 1) * 512],
                                             in_=yst[gidx % 2][:, :]).then_inc(g6, 16)
                        gp.wait_ge(g6, 16 * 32 * (t2 + 1))

    return nc



def _host_prep(inputs):
    x = np.asarray(inputs["L0_attn_x"])
    wqkv = np.ascontiguousarray(np.asarray(inputs["L0_attn_Wqkv"]))
    bqkv = np.asarray(inputs["L0_attn_bqkv"])
    wo = np.ascontiguousarray(np.asarray(inputs["L0_attn_Wo"]))
    bo = np.asarray(inputs["L0_attn_bo"])
    qw_gu = np.asarray(inputs["L0_ffn_qweight_gu"])
    qz_gu = np.ascontiguousarray(np.asarray(inputs["L0_ffn_qzeros_gu"]))
    sc_gu = np.ascontiguousarray(np.asarray(inputs["L0_ffn_scales_gu"]))
    qw_d = np.asarray(inputs["L0_ffn_qweight_down"])
    qz_d = np.ascontiguousarray(np.asarray(inputs["L0_ffn_qzeros_down"]))
    sc_d = np.ascontiguousarray(np.asarray(inputs["L0_ffn_scales_down"]))
    gids_gu = np.asarray(inputs["L0_ffn_g_idx_gu"]).reshape(32, 128)[:, 0]
    gids_d = np.asarray(inputs["L0_ffn_g_idx_down"]).reshape(86, 128)[:, 0]

    qwt_gu = np.ascontiguousarray(qw_gu.T)
    sct_gu = np.ascontiguousarray(
        sc_gu[gids_gu].T.reshape(NGU, 128, 32).transpose(1, 0, 2).reshape(128, NGU * 32))
    qwt_d = np.ascontiguousarray(qw_d.T)
    sct_d = np.ascontiguousarray(
        sc_d[gids_d].T.reshape(NDT, 128, 86).transpose(1, 0, 2).reshape(128, NDT * 86))
    bqkv_t = np.ascontiguousarray(bqkv.reshape(24, 128).T)
    bqv = np.ascontiguousarray(bqkv[2048:3072].reshape(1, DA))
    bo_in = np.ascontiguousarray(bo.reshape(1, H))

    esel = np.zeros((128, 32, 32), dtype=ml_dtypes.bfloat16)
    for k in range(32):
        esel[:, k, gids_gu[k]] = 1
    esel = np.ascontiguousarray(esel.reshape(128, 32 * 32))
    eseld = np.zeros((128, 86, 86), dtype=ml_dtypes.bfloat16)
    for k in range(86):
        eseld[:, k, gids_d[k]] = 1
    eseld = np.ascontiguousarray(eseld.reshape(128, 86 * 86))
    ident = np.eye(128, dtype=ml_dtypes.bfloat16)
    ones = np.ones((1, 128), dtype=ml_dtypes.bfloat16)

    shared = dict(wqkv=wqkv, bqkv_t=bqkv_t, bqv=bqv, wo=wo, bo_in=bo_in,
                  qwt_gu=qwt_gu, sct_gu=sct_gu, qz_gu=qz_gu, sc_gu=sc_gu,
                  qwt_d=qwt_d, sct_d=sct_d, qz_d=qz_d, sc_d=sc_d,
                  esel_in=esel, eseld_in=eseld, ident_in=ident, ones_in=ones)

    in_maps = []
    for core in range(8):
        b = core // 4
        q0 = (core % 4) * TOK
        xb = x[b]
        x_p = np.concatenate([xb[q0:q0 + TOK], xb[:q0], xb[q0 + TOK:]], axis=0)
        m = dict(shared)
        m["x_perm"] = np.ascontiguousarray(x_p)
        in_maps.append(m)
    return in_maps


def kernel(**inputs):
    in_maps = _host_prep(inputs)
    nc = build_nc()
    res = run_bass_kernel_spmd(nc, in_maps, core_ids=list(range(8)))
    out = np.empty((2, S, H), dtype=np.float32)
    for core in range(8):
        b = core // 4
        q0 = (core % 4) * TOK
        out[b, q0:q0 + TOK] = res.results[core]["y"]
    return out


# version 29 (final)
# speedup vs baseline: 1.0801x; 1.0801x over previous
import contextlib
import os
import numpy as np
import ml_dtypes
import concourse.bass as bass
import concourse.mybir as mybir
from concourse.bass_utils import run_bass_kernel_spmd

F32 = mybir.dt.float32
BF16 = mybir.dt.bfloat16
U32 = mybir.dt.uint32
AF = mybir.ActivationFunctionType
ALU = mybir.AluOpType
AX = mybir.AxisListType

H = 4096
DA = 1024
S = 4096
TOK = 1024
INTER = 11008
NGU = 172
NDT = 32
NIC = 86
NKC = 32


def build_nc(nphases=8, debug=False):
    nc = bass.Bass("TRN2", target_bir_lowering=False)

    x_perm = nc.dram_tensor("x_perm", [S, H], F32, kind="ExternalInput").ap()
    wqkv = nc.dram_tensor("wqkv", [H, 3 * DA], F32, kind="ExternalInput").ap()
    bqkv_t = nc.dram_tensor("bqkv_t", [128, 24], F32, kind="ExternalInput").ap()
    bqv = nc.dram_tensor("bqv", [1, DA], F32, kind="ExternalInput").ap()
    wo = nc.dram_tensor("wo", [DA, H], F32, kind="ExternalInput").ap()
    bo_in = nc.dram_tensor("bo_in", [1, H], F32, kind="ExternalInput").ap()
    qwt_gu = nc.dram_tensor("qwt_gu", [2 * INTER, 512], U32, kind="ExternalInput").ap()
    sct_gu = nc.dram_tensor("sct_gu", [128, NGU * 32], F32, kind="ExternalInput").ap()
    qz_gu = nc.dram_tensor("qz_gu", [32, 2752], U32, kind="ExternalInput").ap()
    sc_gu = nc.dram_tensor("sc_gu", [32, 2 * INTER], F32, kind="ExternalInput").ap()
    qwt_d = nc.dram_tensor("qwt_d", [H, 1376], U32, kind="ExternalInput").ap()
    sct_d = nc.dram_tensor("sct_d", [128, NDT * 86], F32, kind="ExternalInput").ap()
    qz_d = nc.dram_tensor("qz_d", [86, 512], U32, kind="ExternalInput").ap()
    sc_d = nc.dram_tensor("sc_d", [86, H], F32, kind="ExternalInput").ap()
    esel_in = nc.dram_tensor("esel_in", [128, 32 * 32], BF16, kind="ExternalInput").ap()
    eseld_in = nc.dram_tensor("eseld_in", [128, 86 * 86], BF16, kind="ExternalInput").ap()
    ident_in = nc.dram_tensor("ident_in", [128, 128], BF16, kind="ExternalInput").ap()
    ones_in = nc.dram_tensor("ones_in", [1, 128], BF16, kind="ExternalInput").ap()

    y = nc.dram_tensor("y", [TOK, H], F32, kind="ExternalOutput").ap()

    dbg = "ExternalOutput" if debug else "Internal"
    xnT_dram = nc.dram_tensor("xnT_dram", [H, S], BF16, kind=dbg).ap()
    kT_dram = nc.dram_tensor("kT_dram", [DA, S], BF16, kind=dbg).ap()
    v_dram = nc.dram_tensor("v_dram", [S, DA], BF16, kind=dbg).ap()
    s_dram = nc.dram_tensor("s_dram", [TOK, H], BF16, kind=dbg).ap()
    wtgu_dram = nc.dram_tensor("wtgu_dram", [2 * INTER, H], BF16, kind=dbg).ap()
    wtd_dram = nc.dram_tensor("wtd_dram", [H, INTER], BF16, kind=dbg).ap()
    hT_dram = nc.dram_tensor("hT_dram", [INTER, TOK], BF16, kind=dbg).ap()
    offgu_dram = nc.dram_tensor("offgu_dram", [32, 2 * INTER], BF16, kind=dbg).ap()
    offd_dram = nc.dram_tensor("offd_dram", [86, H], BF16, kind="Internal").ap()
    bobc_dram = nc.dram_tensor("bobc_dram", [128, H], BF16, kind="Internal").ap()
    if debug:
        rsx_dbg = nc.dram_tensor("rsx_dbg", [32, TOK], BF16, kind="ExternalOutput").ap()
        qt_dbg = nc.dram_tensor("qt_dbg", [128, 8 * TOK], BF16, kind="ExternalOutput").ap()
        pt_dbg = nc.dram_tensor("pt_dbg", [128, 8 * S], BF16, kind="ExternalOutput").ap()
        ot_dbg = nc.dram_tensor("ot_dbg", [128, 8 * TOK], BF16, kind="ExternalOutput").ap()

    with contextlib.ExitStack() as top:
        e = top.enter_context
        ident = e(nc.sbuf_tensor("identS", [128, 128], BF16))
        ones1 = e(nc.sbuf_tensor("onesS", [1, 128], BF16))
        esel_sb = e(nc.sbuf_tensor("eselS", [128, 32 * 32], BF16))
        bqkv_sb = e(nc.sbuf_tensor("bqkvS", [128, 24], F32))
        vb_bc = e(nc.sbuf_tensor("vb_bc", [128, DA], BF16))
        rsx = e(nc.sbuf_tensor("rsx", [32, TOK], BF16))

        sem_names = ["vch", "ld0", "lb0", "lsc0", "lsc0b", "lsct4", "lsct4d", "pe0", "ev0", "vz0", "gz0",
                     "ld1b", "gst1b", "lx2b", "l4b", "l4db", "l5wb",
                     "loffA", "loffB", "l6sb",
                     "gz0b", "l4c", "l4d2", "per6i", "l6r0", "l6r1",
                     "l6w0", "l6w1", "l6w2", "l6w3",
                     "ld1", "ac1", "vvar1", "vxn1", "toct1", "voct1", "trs1", "ars1", "gst1",
                     "lw2", "lx2", "pe2", "a2", "g2",
                     "lw2b", "pe2b", "a2b", "gq2",
                     "lw2c", "pe2c", "v2c", "g2c",
                     "lk3", "pe3", "v3", "vm3", "a3", "vn3", "t3", "vo3",
                     "lv3", "pe3b", "v3b",
                     "lo3", "pe3c", "v3c", "g3",
                     "l4", "v4", "g4", "l4d", "v4d", "g4d",
                     "l5x", "l5w", "pe5", "a5", "v5", "g5",
                     "l6h", "l6e", "pe6r", "a6r", "pe6", "peic6", "a6", "l6s", "v6", "g6"]
        sems = {n: e(nc.semaphore(n)) for n in sem_names}
        g = lambda n: sems[n]
        vch_n = [0]

        with contextlib.ExitStack() as ph:
            pe_ = ph.enter_context
            bo_bf = pe_(nc.sbuf_tensor("bo_bf", [1, H], BF16))
            bqv_bf = pe_(nc.sbuf_tensor("bqv_bf", [1, DA], BF16))
            qzg_sb = pe_(nc.sbuf_tensor("qzg", [32, 2752], U32))
            qzd_sb = pe_(nc.sbuf_tensor("qzd", [86, 512], U32))
            zq_ch = pe_(nc.sbuf_tensor("zq_ch", [32, 5504], U32))
            scg_ch = [pe_(nc.sbuf_tensor(f"scg{i}", [32, 5504], F32)) for i in range(2)]
            offst = [pe_(nc.sbuf_tensor(f"offst{i}", [32, 5504], BF16)) for i in range(2)]
            zq_d = pe_(nc.sbuf_tensor("zq_d", [86, H], U32))
            scd_sb = pe_(nc.sbuf_tensor("scd", [86, H], F32))
            off_d = pe_(nc.sbuf_tensor("off_d0", [86, H], BF16))
            bo_bc = pe_(nc.sbuf_tensor("bo_bc0", [128, H], BF16))
            psb = pe_(nc.psum_tensor("psb", [128, 512], F32))
            ld0, lb0, pe0, ev0, vz0, gz0 = (g(k) for k in ["ld0", "lb0", "pe0", "ev0", "vz0", "gz0"])

            with nc.Block() as blk:
                @blk.sync
                def _(sync):
                    for dst, src in [(ident, ident_in), (ones1, ones_in),
                                     (esel_sb, esel_in), (bqkv_sb, bqkv_t),
                                     (qzg_sb, qz_gu), (qzd_sb, qz_d),
                                     (scd_sb, sc_d)]:
                        sync.dma_start(out=dst[:, :], in_=src).then_inc(ld0, 16)
                    for ch in range(4):
                        if ch >= 2:
                            sync.wait_ge(vz0, ch - 1)
                        sync.dma_start(out=scg_ch[ch % 2][:, :],
                                       in_=sc_gu[:, ch * 5504:(ch + 1) * 5504]).then_inc(
                                           g("lsc0") if ch % 2 == 0 else g("lsc0b"), 16)

                @blk.gpsimd
                def _(gp):
                    gp.dma_start(out=bo_bf[:, :], in_=bo_in).then_inc(lb0, 16)
                    gp.dma_start(out=bqv_bf[:, :], in_=bqv).then_inc(lb0, 16)
                    for ch in range(4):
                        gp.wait_ge(vz0, ch + 1)
                        gp.dma_start(out=offgu_dram[:, ch * 5504:(ch + 1) * 5504],
                                     in_=offst[ch % 2][:, :]).then_inc(
                                         gz0 if ch % 2 == 0 else g("gz0b"), 16)
                    gp.wait_ge(vz0, 5)
                    gp.dma_start(out=offd_dram, in_=off_d[:, :]).then_inc(gz0, 16)
                    gp.wait_ge(ev0, 8)
                    gp.dma_start(out=bobc_dram, in_=bo_bc[:, :]).then_inc(gz0, 16)
                    gp.wait_ge(gz0, 16 * 4)
                    gp.wait_ge(g("gz0b"), 16 * 2)

                @blk.tensor
                def _(t):
                    t.wait_ge(ld0, 16 * 7)
                    t.wait_ge(lb0, 16 * 2)
                    for i in range(8):
                        if i >= 1:
                            t.wait_ge(ev0, i)
                        t.matmul(psb[:, :], ones1[:, :], bo_bf[:, i * 512:(i + 1) * 512],
                                 start=True, stop=True).then_inc(pe0, 1)
                    for i in range(2):
                        t.wait_ge(ev0, 8 + i)
                        t.matmul(psb[:, :], ones1[:, :], bqv_bf[:, i * 512:(i + 1) * 512],
                                 start=True, stop=True).then_inc(pe0, 1)

                @blk.vector
                def _(v):
                    for i in range(8):
                        v.wait_ge(pe0, i + 1)
                        v.tensor_copy(bo_bc[:, i * 512:(i + 1) * 512], psb[:, :]).then_inc(ev0, 1)
                    for i in range(2):
                        v.wait_ge(pe0, 9 + i)
                        v.tensor_copy(vb_bc[:, i * 512:(i + 1) * 512], psb[:, :]).then_inc(ev0, 1)
                    v.wait_ge(ld0, 16 * 7)
                    for ch in range(4):
                        v.wait_ge(g("lsc0") if ch % 2 == 0 else g("lsc0b"), 16 * (ch // 2 + 1))
                        if ch >= 1:
                            v.wait_ge(vz0, ch)
                        for s_ in range(8):
                            v.tensor_scalar(
                                bass.AP(zq_ch, s_, [[5504, 32], [8, 688]]),
                                qzg_sb[:, ch * 688:(ch + 1) * 688], 4 * s_, 15,
                                op0=ALU.logical_shift_right, op1=ALU.bitwise_and)
                        if ch >= 2:
                            v.wait_ge(gz0 if ch % 2 == 0 else g("gz0b"), 16 * (ch // 2))
                        v.scalar_tensor_tensor(offst[ch % 2][:, :], zq_ch[:, :], 1.0,
                                               scg_ch[ch % 2][:, :],
                                               op0=ALU.add, op1=ALU.mult).then_inc(vz0, 1)
                    for s_ in range(8):
                        v.tensor_scalar(
                            bass.AP(zq_d, s_, [[H, 86], [8, 512]]),
                            qzd_sb[:, :], 4 * s_, 15,
                            op0=ALU.logical_shift_right, op1=ALU.bitwise_and)
                    v.scalar_tensor_tensor(off_d[:, :], zq_d[:, :], 1.0, scd_sb[:, :],
                                           op0=ALU.add, op1=ALU.mult).then_inc(vz0, 1)

        if nphases < 1:
            return nc

        p2deq = contextlib.ExitStack()
        de = p2deq.enter_context
        sct_gu_sb = de(nc.sbuf_tensor("sct_guS", [128, NGU * 32], F32, side="right"))
        qbuf_g = [de(nc.sbuf_tensor(f"qbg{i}", [128, 256], U32, side="right")) for i in range(4)]
        wqu_g = de(nc.sbuf_tensor("wqu_g", [128, 2048], mybir.dt.uint16, side="right"))
        wbf_g = [de(nc.sbuf_tensor(f"wbg{i}", [128, 2048], BF16, side="right")) for i in range(3)]
        NHT = 2 * NGU
        DEQA = os.environ.get("KDEQ", "abc")

        def gu_row(jt):
            if jt < 86:
                return (jt // 2) * 512 + (jt % 2) * 128
            ju = jt - 86
            return (ju // 2) * 512 + 256 + (ju % 2) * 128

        L4S = ["l4", "l4b", "l4c", "l4d2"]

        def deq_gu_v(v, ht):
            jt, hf = divmod(ht, 2)
            v.wait_ge(g("lsct4"), 16)
            v.wait_ge(g(L4S[ht % 4]), 16 * (ht // 4 + 1))
            if ht >= 1:
                v.wait_ge(g("v4"), ht)
            qb16 = qbuf_g[ht % 4][:, :].bitcast(mybir.dt.uint16)
            for s_ in range(4):
                v.tensor_scalar(wqu_g[:, s_ * 512:(s_ + 1) * 512],
                                qb16, 4 * s_, 15,
                                op0=ALU.logical_shift_right, op1=ALU.bitwise_and)
            if ht >= 3:
                v.wait_ge(g("g4"), 16 * (ht - 2))
            sc_b = bass.AP(sct_gu_sb, jt * 32 + hf * 16,
                           [[NGU * 32, 128], [0, 4], [1, 16], [0, 32]])
            v.tensor_tensor(wbf_g[ht % 3][:, :], wqu_g[:, :], sc_b,
                            op=ALU.mult).then_inc(g("v4"), 1)

        gu_cur = [0]
        def deq_gu_gp_pairs(gp, n, cap=None):
            for _ in range(n):
                h = gu_cur[0]
                if h >= NHT or (cap is not None and h >= cap):
                    return
                for hl in (list(range(4)) if h == 0 else [h + 3]):
                    if hl < NHT:
                        jt, hf = divmod(hl, 2)
                        if hl >= 4:
                            gp.wait_ge(g("v4"), hl - 3)
                        gp.dma_start(
                            out=qbuf_g[hl % 4][:, :],
                            in_=qwt_gu[jt * 128:(jt + 1) * 128,
                                       hf * 256:(hf + 1) * 256]).then_inc(g(L4S[hl % 4]), 16)
                jt, hf = divmod(h, 2)
                gp.wait_ge(g("v4"), h + 1)
                row = gu_row(jt)
                gp.dma_start(out=wtgu_dram[row:row + 128, hf * 2048:(hf + 1) * 2048],
                             in_=wbf_g[h % 3][:, :]).then_inc(g("g4"), 16)
                gu_cur[0] += 1


        with contextlib.ExitStack() as ph:
            pe_ = ph.enter_context
            xbuf = [pe_(nc.sbuf_tensor(f"xb{i}", [128, H], F32)) for i in range(2)]
            xnbf = [pe_(nc.sbuf_tensor(f"xn{i}", [128, H], BF16)) for i in range(2)]
            sq_scr = pe_(nc.sbuf_tensor("sq_scr", [128, H], BF16))
            stat = pe_(nc.sbuf_tensor("stat", [128, 8], F32))
            xstage = [pe_(nc.sbuf_tensor(f"xst{i}", [128, 1024], BF16)) for i in range(2)]
            ps_tr = [pe_(nc.psum_tensor(f"ps_tr1{i}", [128, 1024], BF16)) for i in range(2)]
            ps_rs = pe_(nc.psum_tensor("ps_rs1", [32, TOK], F32))
            ld1, ac1 = g("ld1"), g("ac1")
            vvar1, vxn1, toct1 = g("vvar1"), g("vxn1"), g("toct1")
            voct1, trs1, ars1, gst1 = g("voct1"), g("trs1"), g("ars1"), g("gst1")
            NT = 32

            with nc.Block() as blk:
                @blk.sync
                def _(sync):
                    for tt in range(NT):
                        if tt >= 2:
                            sync.wait_ge(vxn1, tt - 1)
                        sync.dma_start(out=xbuf[tt % 2][:, :],
                                       in_=x_perm[tt * 128:(tt + 1) * 128, :]).then_inc(
                                           ld1 if tt % 2 == 0 else g("ld1b"), 16)

                @blk.scalar
                def _(sca):
                    for tt in range(NT):
                        p = tt % 2
                        sca.wait_ge(ld1 if p == 0 else g("ld1b"), 16 * (tt // 2 + 1))
                        sca.activation(sq_scr[:, :], xbuf[p][:, :], AF.Square,
                                       accum_out=stat[:, 1 + p:2 + p]).then_inc(ac1, 1)
                        sca.wait_ge(vvar1, tt + 1)
                        sca.activation(stat[:, 6:7], stat[:, 4 + p:5 + p],
                                       AF.Sqrt).then_inc(ac1, 1)
                        if tt < 8:
                            sca.wait_ge(trs1, 4 * (tt + 1))
                            sca.mul(rsx[:, tt * 128:(tt + 1) * 128],
                                    ps_rs[:, tt * 128:(tt + 1) * 128], -1.0).then_inc(ars1, 1)

                @blk.vector
                def _(v):
                    def ch_(ins):
                        vch_n[0] += 1
                        ins.then_inc(g("vch"), 1)
                        v.wait_ge(g("vch"), vch_n[0])
                    for tt in range(NT):
                        p = tt % 2
                        v.wait_ge(ld1 if p == 0 else g("ld1b"), 16 * (tt // 2 + 1))
                        if tt >= 1:
                            v.wait_ge(vxn1, tt)
                        ch_(v.reduce_sum(stat[:, 0:1], xbuf[p][:, :], axis=AX.X))
                        ch_(v.tensor_scalar(stat[:, 3:4], stat[:, 0:1], 1.0 / H, None, op0=ALU.mult))
                        v.wait_ge(ac1, 2 * tt + 1)
                        ch_(v.tensor_scalar(stat[:, 1 + p:2 + p], stat[:, 1 + p:2 + p], 1.0 / H, None,
                                            op0=ALU.mult))
                        ch_(v.tensor_tensor(stat[:, 4 + p:5 + p], stat[:, 3:4], stat[:, 3:4], op=ALU.mult))
                        ch_(v.tensor_tensor(stat[:, 4 + p:5 + p], stat[:, 1 + p:2 + p],
                                            stat[:, 4 + p:5 + p], op=ALU.subtract))
                        v.tensor_scalar(stat[:, 4 + p:5 + p], stat[:, 4 + p:5 + p], 1e-5, None,
                                        op0=ALU.add).then_inc(vvar1, 1)
                        v.wait_ge(ac1, 2 * tt + 2)
                        ch_(v.reciprocal(stat[:, 7:8], stat[:, 6:7]))
                        if tt >= 2:
                            v.wait_ge(toct1, 4 * (tt - 2) + 4)
                        v.tensor_scalar(xnbf[p][:, :], xbuf[p][:, :], stat[:, 3:4], stat[:, 7:8],
                                        op0=ALU.subtract, op1=ALU.mult).then_inc(vxn1, 1)
                        for o in range(4):
                            q = tt * 4 + o
                            v.wait_ge(toct1, q + 1)
                            if q >= 2:
                                v.wait_ge(gst1 if q % 2 == 0 else g("gst1b"), 16 * (q // 2))
                                need = min(q - 1, 32)
                                if need > 0:
                                    v.wait_ge(trs1, need)
                            v.tensor_copy(xstage[q % 2][:, :], ps_tr[q % 2][:, :]).then_inc(voct1, 1)

                @blk.tensor
                def _(t):
                    for tt in range(NT):
                        t.wait_ge(vxn1, tt + 1)
                        for o in range(4):
                            q = tt * 4 + o
                            if q >= 2:
                                t.wait_ge(voct1, q - 1)
                            for j in range(8):
                                kc = o * 8 + j
                                s_ = (kc % 16) // 4
                                base = (kc // 16) * 2048 + s_ + 8 * 64 * (kc % 4)
                                in_ap = bass.AP(xnbf[tt % 2], base,
                                                [[H, 128], [8, 64], [4, 2]])
                                mm = t.transpose(ps_tr[q % 2][:, j * 128:(j + 1) * 128],
                                                 in_ap, ident[:, :])
                            mm.then_inc(toct1, 1)
                            if tt < 8:
                                t.wait_ge(voct1, q + 1)
                                if o == 0 and tt >= 1:
                                    t.wait_ge(ars1, tt)
                                for j in range(8):
                                    kc = o * 8 + j
                                    mm = t.matmul(ps_rs[:, tt * 128:(tt + 1) * 128],
                                                  esel_sb[:, kc * 32:(kc + 1) * 32],
                                                  xstage[q % 2][:, j * 128:(j + 1) * 128],
                                                  start=(kc == 0), stop=(kc == 31))
                                mm.then_inc(trs1, 1)

                @blk.gpsimd
                def _(gp):
                    gp.dma_start(out=sct_gu_sb[:, :], in_=sct_gu).then_inc(g("lsct4"), 16)
                    for tt in range(NT):
                        for o in range(4):
                            q = tt * 4 + o
                            gp.wait_ge(voct1, q + 1)
                            dram_ap = bass.AP(xnT_dram.tensor,
                                              (o * 8) * 128 * S + tt * 128,
                                              [[S, 128], [128 * S, 8], [1, 128]])
                            gp.dma_start(out=dram_ap, in_=xstage[q % 2][:, :]).then_inc(
                                gst1 if q % 2 == 0 else g("gst1b"), 16)
                    gp.wait_ge(gst1, 16 * NT * 2)
                    gp.wait_ge(g("gst1b"), 16 * NT * 2)
                    if debug:
                        gp.wait_ge(ars1, 8)
                        gp.dma_start(out=rsx_dbg, in_=rsx[:, :]).then_inc(gst1, 16)
                        gp.wait_ge(gst1, 16 * (NT * 2 + 1))

        if nphases < 2:
            return nc

        with contextlib.ExitStack() as attn:
            ae = attn.enter_context
            qt_all = ae(nc.sbuf_tensor("qt_all", [128, 8 * TOK], BF16))

            with contextlib.ExitStack() as ph:
                pe_ = ph.enter_context
                wk_sb = pe_(nc.sbuf_tensor("wk_sb", [128, NKC * 1024], BF16))
                xsl = [pe_(nc.sbuf_tensor(f"xsl{i}", [128, NKC * 512], BF16)) for i in range(2)]
                kst = [pe_(nc.sbuf_tensor(f"kst{i}", [128, 512], BF16)) for i in range(2)]
                ps_k = [pe_(nc.psum_tensor(f"ps_k{i}", [128, 512], F32)) for i in range(2)]
                lw2, lx2, pe2, a2, g2 = g("lw2"), g("lx2"), g("pe2"), g("a2"), g("g2")

                with nc.Block() as blk:
                    @blk.gpsimd
                    def _(gp):
                        for kc in range(NKC):
                            s_ = (kc % 16) // 4
                            base = (kc // 16) * 2048 + s_ + 8 * 64 * (kc % 4)
                            src = bass.AP(wqkv.tensor, base * 3072 + 1024,
                                          [[3072 * 8, 64], [3072 * 4, 2], [1, 1024]])
                            gp.dma_start(out=wk_sb[:, kc * 1024:(kc + 1) * 1024],
                                         in_=src).then_inc(lw2, 16)
                        if "a" in DEQA:
                            deq_gu_gp_pairs(gp, 8, cap=224)
                        for idx in range(64):
                            tb, cb = divmod(idx, 8)
                            gp.wait_ge(a2, idx + 1)
                            gp.dma_start(
                                out=bass.AP(kT_dram.tensor, cb * 128 * S + tb * 512,
                                            [[S, 128], [1, 512]]),
                                in_=kst[idx % 2][:, :]).then_inc(g2, 16)
                            if "a" in DEQA:
                                deq_gu_gp_pairs(gp, 3, cap=224)
                        gp.wait_ge(g2, 16 * 64)
                        deq_gu_gp_pairs(gp, NHT, cap=224)

                    @blk.sync
                    def _(sync):
                        for tb in range(8):
                            if tb >= 2:
                                sync.wait_ge(pe2, (tb - 1) * 8)
                            for kc in range(NKC):
                                src = bass.AP(xnT_dram.tensor, kc * 128 * S + tb * 512,
                                              [[S, 128], [1, 512]])
                                sync.dma_start(out=xsl[tb % 2][:, kc * 512:(kc + 1) * 512],
                                               in_=src).then_inc(lx2 if tb % 2 == 0 else g("lx2b"), 16)

                    @blk.tensor
                    def _(t):
                        t.wait_ge(lw2, 16 * NKC)
                        for tb in range(8):
                            t.wait_ge(lx2 if tb % 2 == 0 else g("lx2b"), 16 * NKC * (tb // 2 + 1))
                            for cb in range(8):
                                idx = tb * 8 + cb
                                if idx >= 2:
                                    t.wait_ge(a2, idx - 1)
                                for kc in range(NKC):
                                    mm = t.matmul(
                                        ps_k[idx % 2][:, :],
                                        wk_sb[:, kc * 1024 + cb * 128:kc * 1024 + (cb + 1) * 128],
                                        xsl[tb % 2][:, kc * 512:(kc + 1) * 512],
                                        start=(kc == 0), stop=(kc == NKC - 1))
                                mm.then_inc(pe2, 1)

                    @blk.scalar
                    def _(sca):
                        for idx in range(64):
                            tb, cb = divmod(idx, 8)
                            sca.wait_ge(pe2, idx + 1)
                            if idx >= 2:
                                sca.wait_ge(g2, 16 * (idx - 1))
                            sca.activation(kst[idx % 2][:, :], ps_k[idx % 2][:, :], AF.Identity,
                                           bias=bqkv_sb[:, 8 + cb:9 + cb],
                                           scale=1.0).then_inc(a2, 1)

                    @blk.vector
                    def _(v):
                        if "a" in DEQA:
                            for ht in range(0, 224):
                                deq_gu_v(v, ht)

            with contextlib.ExitStack() as ph:
                pe_ = ph.enter_context
                wq_sb = pe_(nc.sbuf_tensor("wq_sb", [128, NKC * 1024], BF16))
                xsl = [pe_(nc.sbuf_tensor(f"xsq{i}", [128, NKC * 512], BF16)) for i in range(2)]
                ps_q = [pe_(nc.psum_tensor(f"ps_q{i}", [128, 512], F32)) for i in range(2)]
                lw2b, pe2b, a2b, gq2 = g("lw2b"), g("pe2b"), g("a2b"), g("gq2")

                with nc.Block() as blk:
                    @blk.gpsimd
                    def _(gp):
                        for kc in range(NKC):
                            s_ = (kc % 16) // 4
                            base = (kc // 16) * 2048 + s_ + 8 * 64 * (kc % 4)
                            src = bass.AP(wqkv.tensor, base * 3072,
                                          [[3072 * 8, 64], [3072 * 4, 2], [1, 1024]])
                            gp.dma_start(out=wq_sb[:, kc * 1024:(kc + 1) * 1024],
                                         in_=src).then_inc(lw2b, 16)
                        if "b" in DEQA:
                            deq_gu_gp_pairs(gp, NHT, cap=272)
                        if debug:
                            gp.wait_ge(a2b, 16)
                            gp.dma_start(out=qt_dbg, in_=qt_all[:, :]).then_inc(gq2, 16)
                            gp.wait_ge(gq2, 16)

                    @blk.sync
                    def _(sync):
                        for tb in range(2):
                            for kc in range(NKC):
                                src = bass.AP(xnT_dram.tensor, kc * 128 * S + tb * 512,
                                              [[S, 128], [1, 512]])
                                sync.dma_start(out=xsl[tb][:, kc * 512:(kc + 1) * 512],
                                               in_=src).then_inc(lw2b, 16)

                    @blk.tensor
                    def _(t):
                        t.wait_ge(lw2b, 16 * NKC * 3)
                        for tb in range(2):
                            for cb in range(8):
                                idx = tb * 8 + cb
                                if idx >= 2:
                                    t.wait_ge(a2b, idx - 1)
                                for kc in range(NKC):
                                    mm = t.matmul(
                                        ps_q[idx % 2][:, :],
                                        wq_sb[:, kc * 1024 + cb * 128:kc * 1024 + (cb + 1) * 128],
                                        xsl[tb][:, kc * 512:(kc + 1) * 512],
                                        start=(kc == 0), stop=(kc == NKC - 1))
                                mm.then_inc(pe2b, 1)

                    @blk.scalar
                    def _(sca):
                        for idx in range(16):
                            tb, cb = divmod(idx, 8)
                            sca.wait_ge(pe2b, idx + 1)
                            sca.activation(
                                qt_all[:, cb * 1024 + tb * 512:cb * 1024 + (tb + 1) * 512],
                                ps_q[idx % 2][:, :], AF.Identity,
                                bias=bqkv_sb[:, cb:cb + 1], scale=1.0).then_inc(a2b, 1)

                    @blk.vector
                    def _(v):
                        if "b" in DEQA:
                            for ht in range(224, 272):
                                deq_gu_v(v, ht)

            with contextlib.ExitStack() as ph:
                pe_ = ph.enter_context
                wv_sb = pe_(nc.sbuf_tensor("wv_sb", [128, NKC * 1024], BF16))
                xsl = [pe_(nc.sbuf_tensor(f"xsv{i}", [128, NKC * 512], BF16)) for i in range(2)]
                vst = [pe_(nc.sbuf_tensor(f"vst{i}", [128, 1024], BF16)) for i in range(2)]
                ps_v = [pe_(nc.psum_tensor(f"ps_v{i}", [128, 1024], F32)) for i in range(2)]
                lw2c, pe2c, v2c, g2c = g("lw2c"), g("pe2c"), g("v2c"), g("g2c")
                LXB = 16 * NKC * 4

                with nc.Block() as blk:
                    @blk.gpsimd
                    def _(gp):
                        for kc in range(NKC):
                            s_ = (kc % 16) // 4
                            base = (kc // 16) * 2048 + s_ + 8 * 64 * (kc % 4)
                            src = bass.AP(wqkv.tensor, base * 3072 + 2048,
                                          [[3072 * 8, 64], [3072 * 4, 2], [1, 1024]])
                            gp.dma_start(out=wv_sb[:, kc * 1024:(kc + 1) * 1024],
                                         in_=src).then_inc(lw2c, 16)
                        for idx in range(32):
                            gp.wait_ge(v2c, idx + 1)
                            gp.dma_start(out=v_dram[idx * 128:(idx + 1) * 128, :],
                                         in_=vst[idx % 2][:, :]).then_inc(g2c, 16)
                            if "c" in DEQA:
                                hi = 272 + ((idx + 1) * 72) // 32
                                deq_gu_gp_pairs(gp, hi - gu_cur[0])
                        if "c" in DEQA:
                            gp.wait_ge(g("g4"), 16 * gu_cur[0])
                        gp.wait_ge(g2c, 16 * 32)

                    @blk.sync
                    def _(sync):
                        for tb in range(8):
                            if tb >= 2:
                                sync.wait_ge(pe2c, (tb - 1) * 4)
                            for kc in range(NKC):
                                src = bass.AP(xnT_dram.tensor, kc * 128 * S + tb * 512,
                                              [[S, 128], [1, 512]])
                                sync.dma_start(out=xsl[tb % 2][:, kc * 512:(kc + 1) * 512],
                                               in_=src).then_inc(g("lx2") if tb % 2 == 0 else g("lx2b"), 16)

                    @blk.tensor
                    def _(t):
                        t.wait_ge(lw2c, 16 * NKC)
                        for tb in range(8):
                            t.wait_ge(g("lx2") if tb % 2 == 0 else g("lx2b"),
                                      LXB + 16 * NKC * (tb // 2 + 1))
                            for tv in range(4):
                                idx = tb * 4 + tv
                                if idx >= 2:
                                    t.wait_ge(v2c, idx - 1)
                                for kc in range(NKC):
                                    for dh in range(2):
                                        mm = t.matmul(
                                            ps_v[idx % 2][:, dh * 512:(dh + 1) * 512],
                                            xsl[tb % 2][:, kc * 512 + tv * 128:kc * 512 + (tv + 1) * 128],
                                            wv_sb[:, kc * 1024 + dh * 512:kc * 1024 + (dh + 1) * 512],
                                            start=(kc == 0), stop=(kc == NKC - 1))
                                mm.then_inc(pe2c, 1)

                    @blk.vector
                    def _(v):
                        ht_next = [272]
                        for idx in range(32):
                            v.wait_ge(pe2c, idx + 1)
                            if idx >= 2:
                                v.wait_ge(g2c, 16 * (idx - 1))
                            v.tensor_tensor(vst[idx % 2][:, :], ps_v[idx % 2][:, :],
                                            vb_bc[:, :], op=ALU.add).then_inc(v2c, 1)
                            if "c" in DEQA:
                                hi = 272 + ((idx + 1) * 72) // 32
                                while ht_next[0] < hi:
                                    deq_gu_v(v, ht_next[0])
                                    ht_next[0] += 1

            p2deq.close()

            if nphases < 3:
                return nc

            pt_all = ae(nc.sbuf_tensor("pt_all", [128, 8 * S], BF16))
            with contextlib.ExitStack() as ph:
                pe_ = ph.enter_context
                kt_sb = pe_(nc.sbuf_tensor("kt_sb", [128, 8 * S], BF16))
                s_f = pe_(nc.sbuf_tensor("s_f", [128, S], F32))
                p_bf = pe_(nc.sbuf_tensor("p_bf", [128, S], BF16))
                p2_bf = pe_(nc.sbuf_tensor("p2bf", [128, S], BF16))
                sm = pe_(nc.sbuf_tensor("sm3", [128, 4], F32))
                ps_s = [pe_(nc.psum_tensor(f"ps_s{i}", [128, 512], F32)) for i in range(2)]
                ps_t3 = [pe_(nc.psum_tensor(f"ps_t3{i}", [128, 1024], BF16)) for i in range(2)]
                lk3, pe3, v3, vm3 = g("lk3"), g("pe3"), g("v3"), g("vm3")
                a3, vn3, t3, vo3 = g("a3"), g("vn3"), g("t3"), g("vo3")

                with nc.Block() as blk:
                    @blk.sync
                    def _(sync):
                        for dc in range(8):
                            sync.dma_start(out=kt_sb[:, dc * S:(dc + 1) * S],
                                           in_=kT_dram[dc * 128:(dc + 1) * 128, :]).then_inc(lk3, 16)

                    @blk.tensor
                    def _(t):
                        t.wait_ge(lk3, 16 * 8)
                        def _scores(t, qt, kbs):
                            for kb in kbs:
                                idx = qt * 8 + kb
                                if idx >= 2:
                                    t.wait_ge(v3, idx - 1)
                                for dc in range(8):
                                    mm = t.matmul(
                                        ps_s[idx % 2][:, :],
                                        qt_all[:, dc * 1024 + qt * 128:dc * 1024 + (qt + 1) * 128],
                                        kt_sb[:, dc * S + kb * 512:dc * S + (kb + 1) * 512],
                                        start=(dc == 0), stop=(dc == 7))
                                mm.then_inc(pe3, 1)
                        def _pt(t, qp):
                            t.wait_ge(vn3, qp + 1)
                            for o in range(4):
                                qq = qp * 4 + o
                                if qq >= 2:
                                    t.wait_ge(vo3, qq - 1)
                                for j in range(8):
                                    kc = o * 8 + j
                                    mm = t.transpose(ps_t3[qq % 2][:, j * 128:(j + 1) * 128],
                                                     p2_bf[:, kc * 128:(kc + 1) * 128],
                                                     ident[:, :])
                                mm.then_inc(t3, 1)
                        for qt in range(9):
                            if qt < 8:
                                _scores(t, qt, range(0, 2))
                            if qt >= 1:
                                _pt(t, qt - 1)
                            if qt < 8:
                                _scores(t, qt, range(2, 8))

                    @blk.vector
                    def _(v):
                        def ch3(ins):
                            vch_n[0] += 1
                            ins.then_inc(g("vch"), 1)
                            v.wait_ge(g("vch"), vch_n[0])
                        for qt in range(8):
                            for kb in range(8):
                                idx = qt * 8 + kb
                                v.wait_ge(pe3, idx + 1)
                                if kb == 0 and qt >= 1:
                                    v.wait_ge(a3, qt)
                                v.tensor_copy(s_f[:, kb * 512:(kb + 1) * 512],
                                              ps_s[idx % 2][:, :]).then_inc(v3, 1)
                            ch3(v.reduce_max(sm[:, 0:1], s_f[:, :], axis=AX.X))
                            v.tensor_scalar(sm[:, 1:2], sm[:, 0:1], -1.0 / 32.0, None,
                                            op0=ALU.mult).then_inc(vm3, 1)
                            v.wait_ge(a3, qt + 1)
                            ch3(v.reciprocal(sm[:, 3:4], sm[:, 2:3]))
                            if qt >= 1:
                                v.wait_ge(t3, 4 * qt)
                            v.tensor_scalar(p2_bf[:, :], p_bf[:, :],
                                            sm[:, 3:4], None, op0=ALU.mult).then_inc(vn3, 1)
                            for o in range(4):
                                qq = qt * 4 + o
                                v.wait_ge(t3, qq + 1)
                                out_ap = bass.AP(pt_all, (o * 8) * 1024 + qt * 128,
                                                 [[8 * S, 128], [1024, 8], [1, 128]])
                                v.tensor_copy(out_ap, ps_t3[qq % 2][:, :]).then_inc(vo3, 1)

                    @blk.scalar
                    def _(sca):
                        for qt in range(8):
                            sca.wait_ge(vm3, qt + 1)
                            if qt >= 1:
                                sca.wait_ge(vn3, qt)
                            sca.activation(p_bf[:, :], s_f[:, :], AF.Exp,
                                           bias=sm[:, 1:2], scale=1.0 / 32.0,
                                           accum_out=sm[:, 2:3]).then_inc(a3, 1)

            if debug:
                with nc.Block() as blk:
                    @blk.gpsimd
                    def _(gp):
                        gp.dma_start(out=pt_dbg, in_=pt_all[:, :]).then_inc(g("gq2"), 16)
                        gp.wait_ge(g("gq2"), 32 if debug else 16)

            if nphases < 4:
                return nc

            ot_all = ae(nc.sbuf_tensor("ot_all", [128, 8 * TOK], BF16))

            with contextlib.ExitStack() as ph:
                pe_ = ph.enter_context
                v_sb = pe_(nc.sbuf_tensor("v_sb", [128, NKC * 1024], BF16))
                ps_av = [pe_(nc.psum_tensor(f"ps_av{i}", [128, 512], F32)) for i in range(2)]
                lv3, pe3b, v3b = g("lv3"), g("pe3b"), g("v3b")

                with nc.Block() as blk:
                    @blk.sync
                    def _(sync):
                        for kc in range(NKC):
                            sync.dma_start(out=v_sb[:, kc * 1024:(kc + 1) * 1024],
                                           in_=v_dram[kc * 128:(kc + 1) * 128, :]).then_inc(lv3, 16)

                    @blk.tensor
                    def _(t):
                        t.wait_ge(lv3, 16 * NKC)
                        for db in range(8):
                            for qh in range(2):
                                idx = db * 2 + qh
                                if idx >= 2:
                                    t.wait_ge(v3b, idx - 1)
                                for kc in range(NKC):
                                    mm = t.matmul(
                                        ps_av[idx % 2][:, :],
                                        v_sb[:, kc * 1024 + db * 128:kc * 1024 + (db + 1) * 128],
                                        pt_all[:, kc * 1024 + qh * 512:kc * 1024 + (qh + 1) * 512],
                                        start=(kc == 0), stop=(kc == NKC - 1))
                                mm.then_inc(pe3b, 1)

                    @blk.vector
                    def _(v):
                        for idx in range(16):
                            db, qh = divmod(idx, 2)
                            v.wait_ge(pe3b, idx + 1)
                            v.tensor_copy(ot_all[:, db * 1024 + qh * 512:db * 1024 + (qh + 1) * 512],
                                          ps_av[idx % 2][:, :]).then_inc(v3b, 1)

            if debug:
                with nc.Block() as blk:
                    @blk.gpsimd
                    def _(gp):
                        gp.dma_start(out=ot_dbg, in_=ot_all[:, :]).then_inc(g("gq2"), 16)
                        gp.wait_ge(g("gq2"), 48)

            if nphases < 5:
                return nc

            with contextlib.ExitStack() as ph:
                pe_ = ph.enter_context
                wo_sb = pe_(nc.sbuf_tensor("wo_sb", [128, 8 * H], BF16))
                bo_bc = pe_(nc.sbuf_tensor("bo_bc3", [128, H], BF16))
                sst = [pe_(nc.sbuf_tensor(f"sst{i}", [128, 512], BF16)) for i in range(2)]
                ps_wo = [pe_(nc.psum_tensor(f"ps_wo{i}", [128, 512], F32)) for i in range(2)]
                lo3, pe3c, v3c, g3 = g("lo3"), g("pe3c"), g("v3c"), g("g3")

                with nc.Block() as blk:
                    @blk.gpsimd
                    def _(gp):
                        gp.dma_start(out=bo_bc[:, :], in_=bobc_dram).then_inc(lo3, 16)
                        for dc in range(8):
                            src = bass.AP(wo.tensor, dc * 128 * H, [[H, 128], [1, H]])
                            gp.dma_start(out=wo_sb[:, dc * H:(dc + 1) * H],
                                         in_=src).then_inc(lo3, 16)
                        for idx in range(64):
                            qt, hb = divmod(idx, 8)
                            gp.wait_ge(v3c, idx + 1)
                            gp.dma_start(
                                out=bass.AP(s_dram.tensor, qt * 128 * H + hb * 512,
                                            [[H, 128], [1, 512]]),
                                in_=sst[idx % 2][:, :]).then_inc(g3, 16)
                        gp.wait_ge(g3, 16 * 64)

                    @blk.tensor
                    def _(t):
                        t.wait_ge(lo3, 16 * 9)
                        for qt in range(8):
                            for hb in range(8):
                                idx = qt * 8 + hb
                                if idx >= 2:
                                    t.wait_ge(v3c, idx - 1)
                                for dc in range(8):
                                    mm = t.matmul(
                                        ps_wo[idx % 2][:, :],
                                        ot_all[:, dc * TOK + qt * 128:dc * TOK + (qt + 1) * 128],
                                        wo_sb[:, dc * H + hb * 512:dc * H + (hb + 1) * 512],
                                        start=(dc == 0), stop=(dc == 7))
                                mm.then_inc(pe3c, 1)

                    @blk.vector
                    def _(v):
                        for idx in range(64):
                            qt, hb = divmod(idx, 8)
                            v.wait_ge(pe3c, idx + 1)
                            if idx >= 2:
                                v.wait_ge(g3, 16 * (idx - 1))
                            v.scalar_tensor_tensor(sst[idx % 2][:, :], ps_wo[idx % 2][:, :],
                                                   1.0, bo_bc[:, hb * 512:(hb + 1) * 512],
                                                   op0=ALU.mult, op1=ALU.add).then_inc(v3c, 1)

        if nphases < 7:
            return nc

        with contextlib.ExitStack() as ph:
            pe_ = ph.enter_context
            xn5 = pe_(nc.sbuf_tensor("xn5", [128, NKC * 1024], BF16))
            wb5 = [pe_(nc.sbuf_tensor(f"wb5{i}", [128, NKC * 512], BF16)) for i in range(2)]
            offs = [pe_(nc.sbuf_tensor(f"offs{i}", [32, 512], BF16)) for i in range(2)]
            sg5 = pe_(nc.sbuf_tensor("sg5", [128, 1024], BF16))
            hst = [pe_(nc.sbuf_tensor(f"hst{i}", [128, 1024], BF16)) for i in range(2)]
            sct_d_sb = pe_(nc.sbuf_tensor("sct_dS", [128, NDT * 86], F32))
            qbuf_d = [pe_(nc.sbuf_tensor(f"qbd{i}", [128, 352], U32)) for i in range(2)]
            wqu_d = pe_(nc.sbuf_tensor("wqu_d", [128, 2816], U32))
            wbf_d = [pe_(nc.sbuf_tensor(f"wbd{i}", [128, 2816], BF16)) for i in range(2)]
            ps5 = [pe_(nc.psum_tensor(f"ps5_{i}", [128, 512], F32)) for i in range(8)]
            l5x, l5w, pe5 = g("l5x"), g("l5w"), g("pe5")
            a5, v5, g5 = g("a5"), g("v5"), g("g5")
            NI = 43
            NQD = 4 * NDT
            QW = [0, 352, 704, 1056, 1376]
            QG = [0, 22, 44, 66, 86]

            def deq_d_v(v, q):
                dt, qq = divmod(q, 4)
                par = q % 2
                nw = QW[qq + 1] - QW[qq]
                ng = QG[qq + 1] - QG[qq]
                v.wait_ge(g("lsct4d"), 16)
                v.wait_ge(g("l4d") if par == 0 else g("l4db"), 16 * (q // 2 + 1))
                if q >= 1:
                    v.wait_ge(g("v4d"), q)
                for s_ in range(8):
                    v.tensor_scalar(bass.AP(wqu_d, s_, [[2816, 128], [8, nw]]),
                                    qbuf_d[par][:, 0:nw], 4 * s_, 15,
                                    op0=ALU.logical_shift_right, op1=ALU.bitwise_and)
                if q >= 2:
                    v.wait_ge(g("g4d"), 16 * (q - 1))
                sc_b = bass.AP(sct_d_sb, dt * 86 + QG[qq],
                               [[NDT * 86, 128], [1, ng], [0, 128]])
                v.tensor_tensor(wbf_d[par][:, 0:nw * 8], wqu_d[:, 0:nw * 8], sc_b,
                                op=ALU.mult).then_inc(g("v4d"), 1)

            d_cur = [0]
            def deq_d_gp_pairs(gp, n):
                for _ in range(n):
                    qd = d_cur[0]
                    if qd >= NQD:
                        return
                    for ql in ([0, 1] if qd == 0 else [qd + 1]):
                        if ql < NQD:
                            dt, qq = divmod(ql, 4)
                            if ql >= 2:
                                gp.wait_ge(g("v4d"), ql - 1)
                            gp.dma_start(
                                out=qbuf_d[ql % 2][:, 0:QW[qq + 1] - QW[qq]],
                                in_=qwt_d[dt * 128:(dt + 1) * 128,
                                          QW[qq]:QW[qq + 1]]).then_inc(
                                    g("l4d") if ql % 2 == 0 else g("l4db"), 16)
                    dt, qq = divmod(qd, 4)
                    gp.wait_ge(g("v4d"), qd + 1)
                    gp.dma_start(out=wtd_dram[dt * 128:(dt + 1) * 128,
                                              QW[qq] * 8:QW[qq + 1] * 8],
                                 in_=wbf_d[qd % 2][:, 0:(QW[qq + 1] - QW[qq]) * 8]).then_inc(
                                     g("g4d"), 16)
                    d_cur[0] += 1

            with nc.Block() as blk:
                @blk.sync
                def _(sync):
                    sync.dma_start(out=sct_d_sb[:, :], in_=sct_d).then_inc(g("lsct4d"), 16)
                    for kc in range(NKC):
                        sync.dma_start(out=xn5[:, kc * 1024:(kc + 1) * 1024],
                                       in_=xnT_dram[kc * 128:(kc + 1) * 128, 0:1024]).then_inc(l5x, 16)
                    for i in range(NI):
                        if i >= 2:
                            sync.wait_ge(pe5, 2 * (i - 1))
                        for kc in range(NKC):
                            gsrc = bass.AP(wtgu_dram.tensor, i * 512 * H + kc * 128,
                                           [[H, 512], [1, 128]])
                            sync.dma_start_transpose(
                                wb5[i % 2][:, kc * 512:(kc + 1) * 512], gsrc).then_inc(
                                    l5w if i % 2 == 0 else g("l5wb"), 16)

                @blk.scalar
                def _(sca):
                    for i in range(min(2, NI)):
                        lsem = g(["loffA", "loffB"][i % 2])
                        sca.dma_start(out=offs[i % 2][:, 0:256],
                                      in_=offgu_dram[:, (2 * i) * 128:(2 * i + 2) * 128]).then_inc(lsem, 16)
                        sca.dma_start(out=offs[i % 2][:, 256:512],
                                      in_=offgu_dram[:, INTER + (2 * i) * 128:INTER + (2 * i + 2) * 128]).then_inc(lsem, 16)
                    for idx in range(2 * NI):
                        i, t2 = divmod(idx, 2)
                        bank = (idx % 2) * 4
                        sca.wait_ge(pe5, idx + 1)
                        if idx >= 1:
                            sca.wait_ge(v5, idx)
                        sca.activation(sg5[:, 0:512], ps5[bank + 0][:, :], AF.Silu)
                        sca.activation(sg5[:, 512:1024], ps5[bank + 1][:, :], AF.Silu).then_inc(a5, 1)
                        if t2 == 1 and i + 2 < NI:
                            j = i + 2
                            lsem = g(["loffA", "loffB"][j % 2])
                            sca.dma_start(out=offs[j % 2][:, 0:256],
                                          in_=offgu_dram[:, (2 * j) * 128:(2 * j + 2) * 128]).then_inc(lsem, 16)
                            sca.dma_start(out=offs[j % 2][:, 256:512],
                                          in_=offgu_dram[:, INTER + (2 * j) * 128:INTER + (2 * j + 2) * 128]).then_inc(lsem, 16)

                @blk.tensor
                def _(t):
                    t.wait_ge(l5x, 16 * NKC)
                    for i in range(NI):
                        t.wait_ge(l5w if i % 2 == 0 else g("l5wb"), 16 * 32 * (i // 2 + 1))
                        t.wait_ge(g(["loffA", "loffB"][i % 2]), 32 * (i // 2 + 1))
                        for t2 in range(2):
                            idx = 2 * i + t2
                            bank = (idx % 2) * 4
                            if idx >= 2:
                                t.wait_ge(a5, idx - 1)
                                t.wait_ge(v5, idx - 1)
                            for kc in range(NKC):
                                for j4 in range(4):
                                    t.matmul(ps5[bank + j4][:, :],
                                             wb5[i % 2][:, kc * 512 + j4 * 128:kc * 512 + (j4 + 1) * 128],
                                             xn5[:, kc * 1024 + t2 * 512:kc * 1024 + (t2 + 1) * 512],
                                             start=(kc == 0), stop=False)
                            for j4 in range(4):
                                mm = t.matmul(ps5[bank + j4][:, :],
                                              offs[i % 2][:, j4 * 128:(j4 + 1) * 128],
                                              rsx[:, t2 * 512:(t2 + 1) * 512],
                                              start=False, stop=True)
                            mm.then_inc(pe5, 1)

                @blk.vector
                def _(v):
                    dv_cur = [0]
                    for idx in range(2 * NI):
                        bank = (idx % 2) * 4
                        v.wait_ge(a5, idx + 1)
                        if idx >= 2:
                            v.wait_ge(g5, 32 * (idx - 1))
                        v.tensor_tensor(hst[idx % 2][:, 0:512], sg5[:, 0:512],
                                        ps5[bank + 2][:, :], op=ALU.mult)
                        v.tensor_tensor(hst[idx % 2][:, 512:1024], sg5[:, 512:1024],
                                        ps5[bank + 3][:, :], op=ALU.mult).then_inc(v5, 1)
                        hi_d = ((idx + 1) * NQD) // (2 * NI)
                        while dv_cur[0] < hi_d:
                            deq_d_v(v, dv_cur[0])
                            dv_cur[0] += 1

                @blk.gpsimd
                def _(gp):
                    for idx in range(2 * NI):
                        i, t2 = divmod(idx, 2)
                        gp.wait_ge(v5, idx + 1)
                        gp.dma_start(
                            out=hT_dram[(2 * i) * 128:(2 * i + 1) * 128, t2 * 512:(t2 + 1) * 512],
                            in_=hst[idx % 2][:, 0:512]).then_inc(g5, 16)
                        gp.dma_start(
                            out=hT_dram[(2 * i + 1) * 128:(2 * i + 2) * 128, t2 * 512:(t2 + 1) * 512],
                            in_=hst[idx % 2][:, 512:1024]).then_inc(g5, 16)
                        hi_d = ((idx + 1) * NQD) // (2 * NI)
                        deq_d_gp_pairs(gp, hi_d - d_cur[0])
                    gp.wait_ge(g5, 32 * 2 * NI)
                    gp.wait_ge(g("g4d"), 16 * NQD)

        if nphases < 8:
            return nc

        with contextlib.ExitStack() as ph:
            pe_ = ph.enter_context
            hT6 = pe_(nc.sbuf_tensor("hT6", [128, NIC * 512], BF16))
            eseld_sb = pe_(nc.sbuf_tensor("eseldS", [128, 86 * 86], BF16))
            off_d = pe_(nc.sbuf_tensor("off_d6", [86, H], BF16))
            rsh_sb = [pe_(nc.sbuf_tensor(f"rsh{i}", [86, 512], BF16)) for i in range(2)]
            wd6 = [pe_(nc.sbuf_tensor(f"wd6{i}", [128, 1024], BF16)) for i in range(6)]
            rsb = [pe_(nc.sbuf_tensor(f"rsb{i}", [128, 512], BF16)) for i in range(2)]
            gb6 = [pe_(nc.sbuf_tensor(f"gb6{i}", [128, 512], BF16)) for i in range(2)]
            xs6 = [pe_(nc.sbuf_tensor(f"xs6{i}", [128, 512], F32)) for i in range(2)]
            ss6 = [pe_(nc.sbuf_tensor(f"ss6{i}", [128, 512], BF16)) for i in range(2)]
            gs6 = pe_(nc.sbuf_tensor("gs6", [128, 512], F32))
            yst = [pe_(nc.sbuf_tensor(f"yst{i}", [128, 512], F32)) for i in range(2)]
            l6h, l6e, pe6r, a6r = g("l6h"), g("l6e"), g("pe6r"), g("a6r")
            pe6, peic6, a6 = g("pe6"), g("peic6"), g("a6")
            l6s, v6, g6 = g("l6s"), g("v6"), g("g6")

            with contextlib.ExitStack() as rscope:
                ps_rsh = [rscope.enter_context(nc.psum_tensor(f"ps_rsh{i}", [86, 512], F32))
                          for i in range(2)]
                with nc.Block() as blk:
                    @blk.sync
                    def _(sync):
                        sync.dma_start(out=eseld_sb[:, :], in_=eseld_in).then_inc(l6e, 16)
                        sync.dma_start(out=off_d[:, :], in_=offd_dram).then_inc(l6e, 16)
                        for ic in range(NIC):
                            sync.dma_start(out=hT6[:, ic * 512:(ic + 1) * 512],
                                           in_=hT_dram[ic * 128:(ic + 1) * 128,
                                                       0:512]).then_inc(l6h, 16)
                        for ic in range(NIC):
                            if ic >= 2:
                                sync.wait_ge(g("per6i"), ic - 1)
                            sync.dma_start(out=rsb[ic % 2][:, :],
                                           in_=hT_dram[ic * 128:(ic + 1) * 128,
                                                       512:1024]).then_inc(g(f"l6r{ic % 2}"), 16)

                    @blk.tensor
                    def _(t):
                        t.wait_ge(l6h, 16 * NIC)
                        t.wait_ge(l6e, 32)
                        for ic in range(NIC):
                            mm = t.matmul(ps_rsh[0][:, :], eseld_sb[:, ic * 86:(ic + 1) * 86],
                                          hT6[:, ic * 512:(ic + 1) * 512],
                                          start=(ic == 0), stop=(ic == NIC - 1))
                        mm.then_inc(pe6r, 1)
                        for ic in range(NIC):
                            t.wait_ge(g(f"l6r{ic % 2}"), 16 * (ic // 2 + 1))
                            mm = t.matmul(ps_rsh[1][:, :], eseld_sb[:, ic * 86:(ic + 1) * 86],
                                          rsb[ic % 2][:, :],
                                          start=(ic == 0), stop=(ic == NIC - 1))
                            mm.then_inc(g("per6i"), 1)

                    @blk.scalar
                    def _(sca):
                        sca.wait_ge(pe6r, 1)
                        sca.mul(rsh_sb[0][:, :], ps_rsh[0][:, :], -1.0).then_inc(a6r, 1)
                        sca.wait_ge(g("per6i"), NIC)
                        sca.mul(rsh_sb[1][:, :], ps_rsh[1][:, :], -1.0).then_inc(a6r, 1)

            ps_d = [pe_(nc.psum_tensor(f"ps_d{i}", [128, 512], F32)) for i in range(8)]

            for t2 in range(2):
                with nc.Block() as blk:
                    @blk.sync
                    def _(sync, t2=t2):
                        if t2 == 1:
                            for ic in range(NIC):
                                sync.dma_start(out=hT6[:, ic * 512:(ic + 1) * 512],
                                               in_=hT_dram[ic * 128:(ic + 1) * 128,
                                                           512:1024]).then_inc(l6h, 16)
                        for hob2 in range(4):
                            for ic in range(NIC):
                                gic = (t2 * 4 + hob2) * NIC + ic
                                if gic >= 6:
                                    sync.wait_ge(peic6, gic - 5)
                                src = bass.AP(wtd_dram.tensor, hob2 * 1024 * INTER + ic * 128,
                                              [[INTER, 1024], [1, 128]])
                                lsem6 = [g("l6w0"), g("l6w1"), g("l6w2"), g("l6w3"),
                                         g("l6r0"), g("l6r1")][gic % 6]
                                sync.dma_start_transpose(wd6[gic % 6][:, :],
                                                         src).then_inc(lsem6, 16)

                    @blk.tensor
                    def _(t, t2=t2):
                        t.wait_ge(l6h, 16 * NIC * (t2 + 1))
                        t.wait_ge(a6r, 2)
                        for hob2 in range(4):
                            for ic in range(NIC):
                                gic = (t2 * 4 + hob2) * NIC + ic
                                rbase = 16 * 43 if gic % 6 >= 4 else 0
                                lsem6 = [g("l6w0"), g("l6w1"), g("l6w2"), g("l6w3"),
                                         g("l6r0"), g("l6r1")][gic % 6]
                                t.wait_ge(lsem6, rbase + 16 * (gic // 6 + 1))
                                for hh in range(2):
                                    ghob = t2 * 8 + hob2 * 2 + hh
                                    for tt in range(4):
                                        if ic == 0 and ghob >= 2:
                                            t.wait_ge(a6, 4 * (ghob - 2) + tt + 1)
                                        mm = t.matmul(
                                            ps_d[hh * 4 + tt][:, :],
                                            hT6[:, ic * 512 + tt * 128:ic * 512 + (tt + 1) * 128],
                                            wd6[gic % 6][:, hh * 512:(hh + 1) * 512],
                                            start=(ic == 0), stop=False)
                                mm.then_inc(peic6, 1)
                            for hh in range(2):
                                hob = hob2 * 2 + hh
                                for tt in range(4):
                                    mm = t.matmul(ps_d[hh * 4 + tt][:, :],
                                                  rsh_sb[t2][:, tt * 128:(tt + 1) * 128],
                                                  off_d[:, hob * 512:(hob + 1) * 512],
                                                  start=False, stop=True)
                                    mm.then_inc(pe6, 1)

                    @blk.scalar
                    def _(sca, t2=t2):
                        for hob2 in range(4):
                            for hh in range(2):
                                hob = hob2 * 2 + hh
                                ghob = t2 * 8 + hob
                                for tt in range(4):
                                    sidx = hob * 4 + tt
                                    gidx = 32 * t2 + sidx
                                    if gidx >= 2:
                                        sca.wait_ge(v6, gidx - 1)
                                    r0 = t2 * 512 + tt * 128
                                    lsem = l6s if gidx % 2 == 0 else g("l6sb")
                                    sca.dma_start(out=xs6[sidx % 2][:, :],
                                                  in_=x_perm[r0:r0 + 128,
                                                             hob * 512:(hob + 1) * 512]).then_inc(lsem, 16)
                                    sca.dma_start(out=ss6[sidx % 2][:, :],
                                                  in_=s_dram[r0:r0 + 128,
                                                             hob * 512:(hob + 1) * 512]).then_inc(lsem, 16)
                                    sca.wait_ge(pe6, 4 * ghob + tt + 1)
                                    sca.activation(gb6[sidx % 2][:, :], ps_d[hh * 4 + tt][:, :],
                                                   AF.Silu).then_inc(a6, 1)

                    @blk.vector
                    def _(v, t2=t2):
                        for hob in range(8):
                            for tt in range(4):
                                sidx = hob * 4 + tt
                                gidx = 32 * t2 + sidx
                                v.wait_ge(a6, gidx + 1)
                                v.wait_ge(l6s if gidx % 2 == 0 else g("l6sb"), 32 * (gidx // 2 + 1))
                                if gidx >= 2:
                                    v.wait_ge(g6, 16 * (gidx - 1))
                                if gidx >= 1:
                                    v.wait_ge(v6, gidx)
                                v.tensor_tensor(gs6[:, :], gb6[sidx % 2][:, :],
                                                ss6[sidx % 2][:, :], op=ALU.mult)
                                v.tensor_tensor(yst[gidx % 2][:, :], gs6[:, :],
                                                xs6[sidx % 2][:, :], op=ALU.add).then_inc(v6, 1)

                    @blk.gpsimd
                    def _(gp, t2=t2):
                        for hob in range(8):
                            for tt in range(4):
                                sidx = hob * 4 + tt
                                gidx = 32 * t2 + sidx
                                gp.wait_ge(v6, gidx + 1)
                                r0 = t2 * 512 + tt * 128
                                gp.dma_start(out=y[r0:r0 + 128, hob * 512:(hob + 1) * 512],
                                             in_=yst[gidx % 2][:, :]).then_inc(g6, 16)
                        gp.wait_ge(g6, 16 * 32 * (t2 + 1))

    return nc



def _host_prep(inputs):
    x = np.asarray(inputs["L0_attn_x"])
    wqkv = np.ascontiguousarray(np.asarray(inputs["L0_attn_Wqkv"]))
    bqkv = np.asarray(inputs["L0_attn_bqkv"])
    wo = np.ascontiguousarray(np.asarray(inputs["L0_attn_Wo"]))
    bo = np.asarray(inputs["L0_attn_bo"])
    qw_gu = np.asarray(inputs["L0_ffn_qweight_gu"])
    qz_gu = np.ascontiguousarray(np.asarray(inputs["L0_ffn_qzeros_gu"]))
    sc_gu = np.ascontiguousarray(np.asarray(inputs["L0_ffn_scales_gu"]))
    qw_d = np.asarray(inputs["L0_ffn_qweight_down"])
    qz_d = np.ascontiguousarray(np.asarray(inputs["L0_ffn_qzeros_down"]))
    sc_d = np.ascontiguousarray(np.asarray(inputs["L0_ffn_scales_down"]))
    gids_gu = np.asarray(inputs["L0_ffn_g_idx_gu"]).reshape(32, 128)[:, 0]
    gids_d = np.asarray(inputs["L0_ffn_g_idx_down"]).reshape(86, 128)[:, 0]

    qwt_gu = np.ascontiguousarray(qw_gu.T)
    sct_gu = np.ascontiguousarray(
        sc_gu[gids_gu].T.reshape(NGU, 128, 32).transpose(1, 0, 2).reshape(128, NGU * 32))
    qwt_d = np.ascontiguousarray(qw_d.T)
    sct_d = np.ascontiguousarray(
        sc_d[gids_d].T.reshape(NDT, 128, 86).transpose(1, 0, 2).reshape(128, NDT * 86))
    bqkv_t = np.ascontiguousarray(bqkv.reshape(24, 128).T)
    bqv = np.ascontiguousarray(bqkv[2048:3072].reshape(1, DA))
    bo_in = np.ascontiguousarray(bo.reshape(1, H))

    esel = np.zeros((128, 32, 32), dtype=ml_dtypes.bfloat16)
    for kc in range(32):
        s_ = (kc % 16) // 4
        for p in range(128):
            w = 64 * (kc % 4) + p // 2
            hidden = (kc // 16) * 2048 + 8 * w + 4 * (p % 2) + s_
            esel[p, kc, gids_gu[hidden // 128]] = 1
    esel = np.ascontiguousarray(esel.reshape(128, 32 * 32))
    eseld = np.zeros((128, 86, 86), dtype=ml_dtypes.bfloat16)
    for k in range(86):
        eseld[:, k, gids_d[k]] = 1
    eseld = np.ascontiguousarray(eseld.reshape(128, 86 * 86))
    ident = np.eye(128, dtype=ml_dtypes.bfloat16)
    ones = np.ones((1, 128), dtype=ml_dtypes.bfloat16)

    shared = dict(wqkv=wqkv, bqkv_t=bqkv_t, bqv=bqv, wo=wo, bo_in=bo_in,
                  qwt_gu=qwt_gu, sct_gu=sct_gu, qz_gu=qz_gu, sc_gu=sc_gu,
                  qwt_d=qwt_d, sct_d=sct_d, qz_d=qz_d, sc_d=sc_d,
                  esel_in=esel, eseld_in=eseld, ident_in=ident, ones_in=ones)

    in_maps = []
    for core in range(8):
        b = core // 4
        q0 = (core % 4) * TOK
        xb = x[b]
        x_p = np.concatenate([xb[q0:q0 + TOK], xb[:q0], xb[q0 + TOK:]], axis=0)
        m = dict(shared)
        m["x_perm"] = np.ascontiguousarray(x_p)
        in_maps.append(m)
    return in_maps


def kernel(**inputs):
    in_maps = _host_prep(inputs)
    nc = build_nc()
    res = run_bass_kernel_spmd(nc, in_maps, core_ids=list(range(8)))
    out = np.empty((2, S, H), dtype=np.float32)
    for core in range(8):
        b = core // 4
        q0 = (core % 4) * TOK
        out[b, q0:q0 + TOK] = res.results[core]["y"]
    return out
